# version 02
# baseline (speedup 1.0000x reference)
import numpy as np
import ml_dtypes
import concourse.bass as bass
import concourse.mybir as mybir
from concourse.bass_utils import run_bass_kernel_spmd

F32 = mybir.dt.float32
BF16 = mybir.dt.bfloat16
AF = mybir.ActivationFunctionType
ALU = mybir.AluOpType
AX = mybir.AxisListType

P = 128
T = 512
DM = 2048
DC = DM // P
FF = 8192
FC = FF // P
NH = 16
S = 2048
NSEQ = 2
TPS = S // T
NT = NSEQ * TPS
NCORES = 8
EPS_LN = 1e-5

UPTO = 3
ATTN_B = True
ATTN_NP = 3
DEBUG = False

DMA_SEMS = ("sx", "scs", "w1a", "w1b", "w2a", "w2b", "stx", "sst", "sc12",
            "cc", "sr", "p0a", "p0b", "p0c", "p0d", "kc", "t1s", "t1",
            "t2s", "t2", "sqw", "wra", "wrb", "sln", "sqf", "skf", "svt",
            "svr", "woa", "wob", "qfa", "qfb", "kfa", "kfb", "snb", "sdbg")


class Plan:
    ENGINES = ("pe", "dve", "act", "sy", "gp")

    def __init__(self):
        self.thunks = {e: [] for e in self.ENGINES}
        self.cnt = {}
        self.last_wait = {}
        self.sems = {}

    def emit(self, eng, fn, waits=(), inc=None):
        wl = []
        for s, v in waits:
            if not v or v <= 0:
                continue
            k = (eng, s)
            if self.last_wait.get(k, 0) >= v:
                continue
            self.last_wait[k] = v
            wl.append((s, v))
        after = None
        if inc:
            sem, n = inc
            self.cnt[sem] = self.cnt.get(sem, 0) + n
            after = self.cnt[sem]
        sems = self.sems

        def thunk(obj, wl=wl, fn=fn, inc=inc):
            for s, v in wl:
                obj.wait_ge(sems[s], v)
            ins = fn(obj)
            if inc:
                ins.then_inc(sems[inc[0]], inc[1])
        self.thunks[eng].append(thunk)
        return after

    def c(self, sem):
        return self.cnt.get(sem, 0)

    def barrier(self):
        snap = {s: v for s, v in self.cnt.items() if v > 0}
        for e in self.ENGINES:
            waits = [(s, v) for s, v in snap.items() if s != e]
            self.emit(e, lambda obj: obj.nop(), waits=waits)


def build_program(upto, debug=False, attn_b=True, attn_np=3):
    nc = bass.Bass("TRN2", target_bir_lowering=False, debug=False)
    plan = Plan()
    E = plan.emit

    def din(name, shape, dtype=F32):
        return nc.dram_tensor(name, shape, dtype, kind="ExternalInput").ap()

    x0f = din("x0f", [NSEQ * DM, S])
    w1t = din("w1t", [DM, FF])
    w2t = din("w2t", [FF, DM])
    b1b = din("b1b", [P, FC])
    b2b = din("b2b", [P, DC])
    wqt = din("wqt", [DM, DM])
    wkt = din("wkt", [DM, DM])
    wvt = din("wvt", [DM, DM])
    wot = din("wot", [DM, DM])
    cost = din("cost", [64, 2, S])
    costq = din("costq", [64, 2, S])
    masks = din("masks", [P, 4, T], BF16)

    yf = nc.dram_tensor("yf", [NSEQ * DM, S], F32, kind="ExternalOutput").ap()

    def dsc(n, sh, dt=F32):
        return nc.dram_tensor(n, sh, dt, kind="Internal").ap()

    x1f = dsc("x1f", [NSEQ * DM, S])
    x2f = dsc("x2f", [NSEQ * DM, S])
    w1q = dsc("w1q", [DM, FF], BF16)
    w2q = dsc("w2q", [FF, DM], BF16)
    ln2f = dsc("ln2f", [NSEQ * DM, S], BF16)
    qfs = dsc("qfs", [NSEQ * NH * P, S], BF16)
    kfs = dsc("kfs", [NSEQ * NH * P, S], BF16)
    vts = dsc("vts", [NSEQ * S, DM], BF16)
    statsd = dsc("statsd", [6, NT * T])
    c12d = dsc("c12d", [6, NT * T])
    miscd = dsc("miscd", [4, 64])

    from contextlib import ExitStack
    ctx = ExitStack()
    sb = lambda n, sh, dt: ctx.enter_context(nc.sbuf_tensor(n, sh, dt))

    xa = sb("xa", [P, DC, T], F32)
    lnb = sb("lnb", [P, DC, T], BF16)
    hst = sb("hst", [P, FC * T], BF16)
    wstr = sb("wstr", [P, FC * T], BF16)
    t1b = sb("t1b", [P, T], F32)
    t2b = sb("t2b", [P, 2, T], F32)
    sqb = sb("sqb", [P, 2, T], BF16)
    c1r = sb("c1r", [P, T], F32)
    c2r = sb("c2r", [P, T], F32)
    onesc = sb("onesc", [P, 1], BF16)
    onescf = sb("onescf", [P, 1], F32)
    onesr = sb("onesr", [1, P], F32)
    epsc = sb("epsc", [P, 1], F32)
    b1c = sb("b1c", [P, FC], F32)
    b2h = sb("b2h", [P, DC], F32)
    thrc = sb("thrc", [P, 2], F32)
    prt = sb("prt", [P, FC + 1], F32)
    smt = sb("smt", [1, 64], F32)
    srow = sb("srow", [1, 2, T], F32)
    crow = sb("crow", [1, 2, T], F32)
    rta = sb("rta", [64, T], F32)
    rtb = sb("rtb", [64, T], F32)
    qkst = sb("qkst", [P, 2, T], BF16)
    nrm = sb("nrm", [1, 4 * NH], F32)
    negB = sb("negB", [P, 2 * NH], F32)
    linvb = sb("linvb", [1, T], F32)
    linvr = sb("linvr", [P, T], F32)

    csb = t2b.rearrange("p a t -> p (a t)").bitcast(F32).rearrange(
        "p (a t) -> p a t", a=2)[0:64, :, :]
    qnsq = sqb[:, 0, :]
    afb = lnb
    ptb = t2b.rearrange("p a t -> p (a t)").bitcast(BF16).rearrange(
        "p (a t) -> p a t", a=4)
    wov = wstr[:, 0:2 * 1024].rearrange("p (a n) -> p a n", a=2)
    kfv = wstr[:, 2 * 1024:2 * 1024 + 2 * S].rearrange("p (a n) -> p a n", a=2)
    _o = 2 * 1024 + 2 * S
    mskb = wstr[:, _o:_o + 4 * T].rearrange("p (a t) -> p a t", a=4)

    w1cb = hst.bitcast(F32).rearrange("p (a f) -> p a f", a=2)
    awb = wstr[:, 0:2 * FF].bitcast(F32)
    qcb = wstr[:, 2 * FF:4 * FF].rearrange("p (a f) -> p a f", a=2)
    sgb = lnb.rearrange("p c t -> p (c t)")
    w2cb = xa.rearrange("p c t -> p (c t)")[:, 0:4096].rearrange(
        "p (a f) -> p a f", a=2)

    w1v = wstr[:, 0:2 * DC * T].rearrange("p (a c f) -> p a c f", a=2, c=DC)
    w2v = wstr[:, 2 * DC * T:4 * DC * T].rearrange(
        "p (a c j) -> p a c j", a=2, c=FC)
    hsv = hst.rearrange("p (c t) -> p c t", c=FC)
    wres = hst.rearrange("p (c n) -> p c n", c=DC)
    wpre = wstr.rearrange("p (c n) -> p c n", c=DC)
    vrs = hst.rearrange("p (c n) -> p c n", c=DC)

    pbank = [ctx.enter_context(nc.psum_tensor(f"pb{i}", [P, T], F32))
             for i in range(8)]
    for e in Plan.ENGINES:
        plan.sems[e] = ctx.enter_context(nc.semaphore(f"sem_{e}"))
    for s in DMA_SEMS:
        plan.sems[s] = ctx.enter_context(nc.semaphore(f"sem_{s}"))

    ev_ones = E("dve", lambda v: v.memset(onesc[:], 1.0), inc=("dve", 1))
    E("dve", lambda v: v.memset(onescf[:], 1.0), inc=("dve", 1))
    E("dve", lambda v: v.memset(onesr[:], 1.0), inc=("dve", 1))
    E("dve", lambda v: v.memset(epsc[:], EPS_LN), inc=("dve", 1))
    E("dve", lambda v: v.memset(nrm[:], 0.0), inc=("dve", 1))
    E("sy", lambda s: s.dma_start(out=b1c[:], in_=b1b), inc=("kc", 16))
    E("sy", lambda s: s.dma_start(out=b2h[:], in_=b2b), inc=("kc", 16))
    ev_kc = plan.c("kc")
    ev_b2h = E("dve", lambda v: v.tensor_scalar_mul(b2h[:], b2h[:], 0.5),
               waits=[("kc", ev_kc)], inc=("dve", 1))

    def quantize(wt, wq, nchunk, width, cbuf, thr_i, nelem, sema, semb,
                 tsem_s, tsem):
        nseg = width // 2048
        ev_dma = [0] * nchunk
        ev_red = [0] * nchunk
        for c in range(nchunk):
            b = c % 2
            ssem = sema if b == 0 else semb
            w = [("dve", ev_red[c - 2])] if c >= 2 else []
            E("sy", lambda s, c=c, b=b: s.dma_start(
                out=cbuf[:, b, 0:width], in_=wt[c * P:(c + 1) * P, :]),
                waits=w, inc=(ssem, 16))
            ev_dma[c] = plan.c(ssem)
            for g in range(nseg):
                ev_red[c] = E("dve", lambda v, c=c, b=b, g=g: v.tensor_reduce(
                    prt[:, c * nseg + g:c * nseg + g + 1],
                    cbuf[:, b, g * 2048:(g + 1) * 2048], AX.X, ALU.add,
                    apply_absolute_value=True),
                    waits=[(ssem, ev_dma[c])], inc=("dve", 1))
        ev_p = E("dve", lambda v: v.tensor_reduce(
            prt[:, FC:FC + 1], prt[:, 0:nchunk * nseg], AX.X, ALU.add),
            inc=("dve", 1))
        ev_tot = E("pe", lambda t: t.matmul(
            pbank[7][0:1, 0:1], onescf[:], prt[:, FC:FC + 1],
            start=True, stop=True),
            waits=[("dve", ev_p)], inc=("pe", 1))
        ev_thr = E("dve", lambda v: v.tensor_scalar(
            smt[0:1, 0:1], pbank[7][0:1, 0:1], 0.5 / nelem, 0.5e-8,
            ALU.mult, ALU.max),
            waits=[("pe", ev_tot)], inc=("dve", 1))
        ev_out = E("sy", lambda s: s.dma_start(
            out=miscd[thr_i:thr_i + 1, 0:1], in_=smt[0:1, 0:1]),
            waits=[("dve", ev_thr)], inc=(tsem_s, 16))
        ev_bc = E("sy", lambda s: s.dma_start(
            out=thrc[:, thr_i:thr_i + 1],
            in_=miscd[thr_i:thr_i + 1, 0:1].broadcast_to((P, 1))),
            waits=[(tsem_s, ev_out)], inc=(tsem, 16))
        ev_ge = [0] * nchunk
        ev_q = [0] * nchunk
        ev_sg = [0] * nchunk
        for c in range(nchunk):
            b = c % 2
            ssem = sema if b == 0 else semb
            w = [(tsem, ev_bc)]
            if c >= 2:
                w.append(("dve", ev_q[c - 2]))
            else:
                w.append(("dve", ev_red[nchunk - 1]))
            E("sy", lambda s, c=c, b=b: s.dma_start(
                out=cbuf[:, b, 0:width], in_=wt[c * P:(c + 1) * P, :]),
                waits=w, inc=(ssem, 16))
            ev_dmab = plan.c(ssem)
            aw = [(ssem, ev_dmab)]
            if c >= 1:
                aw.append(("dve", ev_ge[c - 1]))
            ev_abs = E("act", lambda a, b=b: a.activation(
                awb[:, 0:width], cbuf[:, b, 0:width], AF.Abs),
                waits=aw, inc=("act", 1))
            sw = [("dve", ev_q[c - 1])] if c >= 1 else []
            ev_sg[c] = E("act", lambda a, b=b: a.activation(
                sgb[:, 0:width], cbuf[:, b, 0:width], AF.Sign),
                waits=sw, inc=("act", 1))
            ev_ge[c] = E("dve", lambda v, b=b, thr_i=thr_i: v.tensor_scalar(
                qcb[:, b, 0:width], awb[:, 0:width], thrc[:, thr_i:thr_i + 1],
                None, ALU.is_ge),
                waits=[("act", ev_abs)], inc=("dve", 1))
            ev_q[c] = E("dve", lambda v, b=b: v.tensor_mul(
                qcb[:, b, 0:width], qcb[:, b, 0:width], sgb[:, 0:width]),
                waits=[("act", ev_sg[c])], inc=("dve", 1))
            E("gp", lambda g, c=c, b=b: g.dma_start(
                out=wq[c * P:(c + 1) * P, :], in_=qcb[:, b, 0:width]),
                waits=[("dve", ev_q[c])], inc=("sqw", 16))
        return plan.c("sqw")


    srow_free = {"ev": 0}

    def finish_stats(ti, ln_idx, ev_mm2, extra=()):
        E("dve", lambda v: v.tensor_scalar_mul(
            srow[0:1, 1, :], pbank[4][0:1, :], 1.0 / DM),
            waits=[("pe", ev_mm2), ("sst", srow_free["ev"])] + list(extra),
            inc=("dve", 1))
        E("dve", lambda v: v.tensor_scalar_mul(
            srow[0:1, 0, :], pbank[5][0:1, :], 1.0 / DM), inc=("dve", 1))
        E("dve", lambda v: v.scalar_tensor_tensor(
            t1b[0:1, 0:T], srow[0:1, 1, :], -1.0, srow[0:1, 1, :],
            ALU.mult, ALU.mult), inc=("dve", 1))
        ev_v2 = E("dve", lambda v: v.tensor_add(
            srow[0:1, 0, :], srow[0:1, 0, :], t1b[0:1, 0:T]), inc=("dve", 1))
        sl = slice(ti * T, (ti + 1) * T)
        ev_o = E("sy", lambda s, sl=sl: s.dma_start(
            out=statsd[2 * ln_idx:2 * ln_idx + 2, sl], in_=srow[0:1, :, :]),
            waits=[("dve", ev_v2)], inc=("sst", 16))
        srow_free["ev"] = ev_o
        return ev_o

    def ln_stats_tile(src, ti, ln_idx, xw=(), kw=0):
        s_, trow = ti // TPS, (ti % TPS) * T
        for c in range(DC):
            E("sy", lambda s, c=c, s_=s_, trow=trow: s.dma_start(
                out=xa[:, c, :],
                in_=src[s_ * DM + c * P:s_ * DM + (c + 1) * P, trow:trow + T]),
                waits=list(xw), inc=("sx", 16))
        ev_x = plan.c("sx")
        ev_mm2 = 0
        ev_sq = [0] * DC
        for c in range(DC):
            ev_cast = E("act", lambda a, c=c: a.activation(
                lnb[:, c, :], xa[:, c, :], AF.Copy),
                waits=[("sx", ev_x), ("pe", kw)], inc=("act", 1))
            evq = E("dve", lambda v, c=c: v.tensor_mul(
                sqb[:, c % 2, :], lnb[:, c, :], lnb[:, c, :]),
                waits=[("act", ev_cast),
                       ("pe", ev_sq[c - 2] if c >= 2 else kw)],
                inc=("dve", 1))
            E("pe", lambda t, c=c: t.matmul(
                pbank[4][0:1, :], onesc[:], lnb[:, c, :],
                start=(c == 0), stop=(c == DC - 1)),
                waits=[("act", ev_cast), ("dve", ev_ones)], inc=("pe", 1))
            ev_sq[c] = ev_mm2 = E("pe", lambda t, c=c: t.matmul(
                pbank[5][0:1, :], onesc[:], sqb[:, c % 2, :],
                start=(c == 0), stop=(c == DC - 1)),
                waits=[("dve", evq)], inc=("pe", 1))
        return finish_stats(ti, ln_idx, ev_mm2)

    def sqrt_epoch(ln_idx, ready):
        ev_last = 0
        for ti in range(NT):
            sl = slice(ti * T, (ti + 1) * T)
            w = [("sst", ready), ("sc12", ev_last),
                 ("dve", plan.c("dve")), ("act", plan.c("act"))]
            E("sy", lambda s, sl=sl: s.dma_start(
                out=srow[0:1, :, :],
                in_=statsd[2 * ln_idx:2 * ln_idx + 2, sl]),
                waits=w, inc=("sr", 16))
            ev_s = E("act", lambda a: a.activation(
                crow[0:1, 0, :], srow[0:1, 0, :], AF.Sqrt, bias=epsc[0:1, :]),
                waits=[("sr", plan.c("sr"))], inc=("act", 1))
            E("dve", lambda v: v.reciprocal(linvb[:], crow[0:1, 0, :]),
              waits=[("act", ev_s)], inc=("dve", 1))
            E("dve", lambda v: v.tensor_copy(crow[0:1, 0, :], linvb[:]),
              inc=("dve", 1))
            ev_c2 = E("dve", lambda v: v.scalar_tensor_tensor(
                crow[0:1, 1, :], srow[0:1, 1, :], -1.0, crow[0:1, 0, :],
                ALU.mult, ALU.mult), inc=("dve", 1))
            ev_last = E("sy", lambda s, sl=sl: s.dma_start(
                out=c12d[2 * ln_idx:2 * ln_idx + 2, sl], in_=crow[0:1, :, :]),
                waits=[("dve", ev_c2)], inc=("sc12", 16))
        return ev_last

    def ln_bcast(ln_idx, ti, ready, warw=(), eng="act"):
        sl = slice(ti * T, (ti + 1) * T)
        E(eng, lambda g, sl=sl: g.dma_start(
            out=c1r[:],
            in_=c12d[2 * ln_idx:2 * ln_idx + 1, sl].broadcast_to((P, T))),
            waits=[("sc12", ready)] + list(warw), inc=("cc", 16))
        E(eng, lambda g, sl=sl: g.dma_start(
            out=c2r[:],
            in_=c12d[2 * ln_idx + 1:2 * ln_idx + 2, sl].broadcast_to((P, T))),
            inc=("cc", 16))
        return plan.c("cc")

    def ln_tile(ev_x, ev_bc, lnw=()):
        evs = [0] * DC
        for c in range(DC):
            w = [("cc", ev_bc), ("sx", ev_x)] + (list(lnw) if c == 0 else [])
            E("dve", lambda v, c=c: v.tensor_mul(
                t1b[:, 0:T], xa[:, c, :], c1r[:]), waits=w, inc=("dve", 1))
            evs[c] = E("dve", lambda v, c=c: v.tensor_add(
                lnb[:, c, :], t1b[:, 0:T], c2r[:]), inc=("dve", 1))
        return evs

    def ffn_phase(src, dst, ln_idx, dst_ln_idx, do_stats, ev_c12_ready):
        ev_gel = {}
        ev_w1 = {0: 0, 1: 0}
        ev_w1pe = {0: 0, 1: 0}
        ev_w2 = {0: 0, 1: 0}
        ev_w2pe = {0: 0, 1: 0}
        ev_add = [0] * DC
        ev_cast = [0] * DC
        ev_sqv = [0] * DC
        ev_s2v = [0] * DC
        prev = {"xa_gp": 0, "xa_act": 0, "lnb_pe": 0, "bc_dve": 0,
                "s45_dve": 0}

        for ti in range(NT):
            s_, trow = ti // TPS, (ti % TPS) * T
            for c in range(DC):
                E("act", lambda s, c=c, s_=s_, trow=trow: s.dma_start(
                    out=xa[:, c, :],
                    in_=src[s_ * DM + c * P:s_ * DM + (c + 1) * P,
                            trow:trow + T]),
                    waits=[("stx", prev["xa_gp"]), ("act", prev["xa_act"]),
                           ("dve", prev["bc_dve"])], inc=("sx", 16))
            ev_x = plan.c("sx")
            ev_bc = ln_bcast(ln_idx, ti, ev_c12_ready,
                             warw=[("dve", prev["bc_dve"])], eng="act")
            ev_ln = ln_tile(ev_x, ev_bc,
                            lnw=[("pe", prev["lnb_pe"]),
                                 ("act", prev["xa_act"])])
            prev["bc_dve"] = ev_ln[-1]

            for g in range(DC):
                sl_ = g % 2
                wsem = "w1a" if sl_ == 0 else "w1b"
                E("sy", lambda s, g=g, sl_=sl_: s.dma_start(
                    out=w1v[:, sl_, :, :],
                    in_=w1q[:, g * T:(g + 1) * T].rearrange(
                        "(c p) f -> p c f", p=P)),
                    waits=[("pe", ev_w1pe[sl_]), ("sqw", ev_wq)],
                    inc=(wsem, 16))
                ev_w1[sl_] = plan.c(wsem)
                for f in range(4):
                    ffc = g * 4 + f
                    bank = (0, 1, 6, 7)[ffc % 4]
                    mw = [(wsem, ev_w1[sl_]), ("dve", ev_ln[DC - 1]),
                          ("act", ev_gel.get(bank, 0))]
                    last = 0
                    for dc in range(DC):
                        last = E("pe", lambda t, f=f, dc=dc, bank=bank,
                                 sl_=sl_: t.matmul(
                                     pbank[bank][:],
                                     w1v[:, sl_, dc, f * P:(f + 1) * P],
                                     lnb[:, dc, :],
                                     start=(dc == 0), stop=(dc == DC - 1)),
                                 waits=(mw if dc == 0 else []), inc=("pe", 1))
                    ev_w1pe[sl_] = last
                    ev_gel[bank] = E("act", lambda a, ffc=ffc, bank=bank:
                                     a.activation(hsv[:, ffc, :],
                                                  pbank[bank][:], AF.Gelu,
                                                  bias=b1c[:, ffc:ffc + 1]),
                                     waits=[("pe", last), ("kc", ev_kc)],
                                     inc=("act", 1))
            ev_gel_all = max(ev_gel[0], ev_gel[1])

            ev_s2 = 0
            for d in range(DC):
                sl_ = d % 2
                wsem = "w2a" if sl_ == 0 else "w2b"
                for q4 in range(4):
                    E("gp", lambda g_, d=d, q4=q4, sl_=sl_: g_.dma_start(
                        out=w2v[:, sl_, q4 * DC:(q4 + 1) * DC, :],
                        in_=w2q[q4 * (FF // 4):(q4 + 1) * (FF // 4),
                                d * P:(d + 1) * P].rearrange(
                            "(c p) j -> p c j", p=P)),
                        waits=([("pe", ev_w2pe[sl_]), ("sqw", ev_wq)]
                               if q4 == 0 else []),
                        inc=(wsem, 16))
                ev_w2[sl_] = plan.c(wsem)
                bank = 2 + sl_
                mw = [(wsem, ev_w2[sl_]), ("act", ev_gel_all),
                      ("dve", ev_add[(d - 2) % DC] if (ti or d >= 2) else 0)]
                last = 0
                for fc in range(FC):
                    last = E("pe", lambda t, fc=fc, bank=bank, sl_=sl_:
                             t.matmul(pbank[bank][:], w2v[:, sl_, fc, :],
                                      hsv[:, fc, :],
                                      start=(fc == 0), stop=(fc == FC - 1)),
                             waits=(mw if fc == 0 else []), inc=("pe", 1))
                ev_w2pe[sl_] = last
                E("dve", lambda v, d=d: v.tensor_scalar_add(
                    xa[:, d, :], xa[:, d, :], b2h[:, d:d + 1]),
                    waits=[("dve", ev_b2h)], inc=("dve", 1))
                ev_add[d] = E("dve", lambda v, d=d, bank=bank:
                              v.scalar_tensor_tensor(
                                  xa[:, d, :], pbank[bank][:], 0.5,
                                  xa[:, d, :], ALU.mult, ALU.add),
                              waits=[("pe", last)], inc=("dve", 1))
                E("gp", lambda g_, d=d, s_=s_, trow=trow: g_.dma_start(
                    out=dst[s_ * DM + d * P:s_ * DM + (d + 1) * P,
                            trow:trow + T],
                    in_=xa[:, d, :]),
                    waits=[("dve", ev_add[d])], inc=("stx", 16))
                if do_stats:
                    cw = [("dve", ev_add[d])]
                    if d == 0:
                        cw += [("dve", prev["s45_dve"]), ("pe", prev["lnb_pe"])]
                    ev_cast[d] = E("act", lambda a, d=d: a.activation(
                        lnb[:, d, :], xa[:, d, :], AF.Copy),
                        waits=cw, inc=("act", 1))
                    ev_sqv[d] = E("dve", lambda v, d=d: v.tensor_mul(
                        sqb[:, d % 2, :], lnb[:, d, :], lnb[:, d, :]),
                        waits=[("act", ev_cast[d]),
                               ("pe", ev_s2v[d - 2] if d >= 2 else 0)],
                        inc=("dve", 1))
                    for dd in ([d - 1] if d else []) + ([d] if d == DC - 1 else []):
                        E("pe", lambda t, dd=dd: t.matmul(
                            pbank[4][0:1, :], onesc[:], lnb[:, dd, :],
                            start=(dd == 0), stop=(dd == DC - 1)),
                            waits=[("act", ev_cast[dd])], inc=("pe", 1))
                        ev_s2 = E("pe", lambda t, dd=dd: t.matmul(
                            pbank[5][0:1, :], onesc[:], sqb[:, dd % 2, :],
                            start=(dd == 0), stop=(dd == DC - 1)),
                            waits=[("dve", ev_sqv[dd])], inc=("pe", 1))
                        ev_s2v[dd] = ev_s2
            if do_stats:
                finish_stats(ti, dst_ln_idx, ev_s2)
                prev["s45_dve"] = plan.c("dve")
            prev["xa_gp"] = plan.c("stx")
            prev["xa_act"] = ev_cast[DC - 1] if do_stats else 0
            prev["lnb_pe"] = plan.c("pe")

    def attn_phase(ev_c12_ready):
        resv = [wres, wpre]
        ressem = ["wra", "wrb"]
        ev_res = [0, 0]
        ev_res_pe = [0, 0]
        wts = [wqt, wkt, wvt]
        prev = {"lnb_pe": 0, "bc_dve": 0, "qk_dma": {0: 0, 1: 0},
                "qk_dve": {0: 0, 1: 0}, "csb_pe": 0, "vdma": 0, "vact": 0,
                "nrm_pe": 0}
        ev_lnst = [0] * NT

        for p_ in range(attn_np):
            rb = p_ % 2
            prev["qk_dma"] = {0: 0, 1: 0}
            rw = [("pe", ev_res_pe[rb])]
            if p_ == 0:
                rw += [("act", plan.c("act")), ("dve", plan.c("dve")),
                       ("pe", plan.c("pe")), ("sqw", ev_wq)]
            E("gp", lambda g, p_=p_, rb=rb: g.dma_start(
                out=resv[rb][:, :, :],
                in_=wts[p_].rearrange("(c p) n -> p c n", p=P)),
                waits=rw, inc=(ressem[rb], 16))
            ev_res[rb] = plan.c(ressem[rb])
            for ti in range(NT):
                s_, trow = ti // TPS, (ti % TPS) * T
                if p_ == 0:
                    for c in range(DC):
                        E("sy", lambda s, c=c, s_=s_, trow=trow: s.dma_start(
                            out=xa[:, c, :],
                            in_=x1f[s_ * DM + c * P:s_ * DM + (c + 1) * P,
                                    trow:trow + T]),
                            waits=[("dve", prev["bc_dve"]),
                                   ("stx", plan.c("stx"))], inc=("sx", 16))
                    ev_x = plan.c("sx")
                    ev_bc = ln_bcast(1, ti, ev_c12_ready,
                                     warw=[("dve", prev["bc_dve"])])
                    ev_ln = ln_tile(ev_x, ev_bc,
                                    lnw=[("pe", prev["lnb_pe"]),
                                         ("sln", ev_lnst[ti - 1]
                                          if ti else 0)])
                    prev["bc_dve"] = ev_ln[-1]
                    ev_ready = ("dve", ev_ln[DC - 1])
                    E("gp", lambda g, s_=s_, trow=trow: g.dma_start(
                        out=ln2f[s_ * DM:(s_ + 1) * DM,
                                 trow:trow + T].rearrange(
                            "(c p) t -> p c t", p=P),
                        in_=lnb[:, :, :]),
                        waits=[("dve", ev_ln[DC - 1])], inc=("sln", 16))
                    ev_lnst[ti] = plan.c("sln")
                else:
                    E("sy", lambda s, s_=s_, trow=trow: s.dma_start(
                        out=lnb[:, :, :],
                        in_=ln2f[s_ * DM:(s_ + 1) * DM,
                                 trow:trow + T].rearrange(
                            "(c p) t -> p c t", p=P)),
                        waits=[("sln", plan.c("sln")),
                               ("pe", prev["lnb_pe"])], inc=("sx", 16))
                    ev_ready = ("sx", plan.c("sx"))
                if p_ < 2:
                    tab = costq if p_ == 0 else cost
                    E("sy", lambda s, tab=tab, trow=trow: s.dma_start(
                        out=csb[:, :, :], in_=tab[:, :, trow:trow + T]),
                        waits=[("pe", prev["csb_pe"]),
                               ("dve", plan.c("dve") if ti == 0 and p_ == 0
                                else 0)],
                        inc=("scs", 16))
                    ev_csb = plan.c("scs")

                    for h in range(NH):
                        qb = h % 2
                        bank = 6 + qb
                        mw = [(ressem[rb], ev_res[rb]), ev_ready,
                              ("dve", prev["qk_dve"][qb])]
                        last = 0
                        for dc in range(DC):
                            last = E("pe", lambda t, h=h, dc=dc, bank=bank,
                                     rb=rb: t.matmul(
                                         pbank[bank][:],
                                         resv[rb][:, dc, h * P:(h + 1) * P],
                                         lnb[:, dc, :],
                                         start=(dc == 0), stop=(dc == DC - 1)),
                                     waits=(mw if dc == 0 else []),
                                     inc=("pe", 1))
                        dsem = "sqf" if p_ == 0 else "skf"
                        rw2 = [("pe", last), ("scs", ev_csb),
                               (dsem, prev["qk_dma"][qb])]
                        E("dve", lambda v, bank=bank: v.tensor_mul(
                            rta[:, :], pbank[bank][0:64, :], csb[:, 0, :]),
                            waits=rw2, inc=("dve", 1))
                        E("dve", lambda v, bank=bank: v.tensor_mul(
                            rtb[:, :], pbank[bank][64:128, :], csb[:, 1, :]),
                            inc=("dve", 1))
                        E("dve", lambda v, qb=qb: v.tensor_sub(
                            qkst[0:64, qb, :], rta[:, :], rtb[:, :]),
                            inc=("dve", 1))
                        E("dve", lambda v, bank=bank: v.tensor_mul(
                            rta[:, :], pbank[bank][0:64, :], csb[:, 1, :]),
                            inc=("dve", 1))
                        E("dve", lambda v, bank=bank: v.tensor_mul(
                            rtb[:, :], pbank[bank][64:128, :], csb[:, 0, :]),
                            inc=("dve", 1))
                        ev_rope = E("dve", lambda v, qb=qb: v.tensor_add(
                            qkst[64:128, qb, :], rta[:, :], rtb[:, :]),
                            inc=("dve", 1))
                        prev["qk_dve"][qb] = ev_rope
                        evn = E("dve", lambda v, qb=qb: v.tensor_mul(
                            qnsq[:, :], qkst[:, qb, :], qkst[:, qb, :]),
                            waits=[("pe", prev["nrm_pe"])], inc=("dve", 1))
                        ev_nm = E("pe", lambda t: t.matmul(
                            pbank[5][0:1, :], onesc[:], qnsq[:, :],
                            start=True, stop=True),
                            waits=[("dve", evn)], inc=("pe", 1))
                        prev["nrm_pe"] = ev_nm
                        E("dve", lambda v: v.tensor_reduce(
                            smt[0:1, 32:33], pbank[5][0:1, :], AX.X, ALU.max),
                            waits=[("pe", ev_nm)], inc=("dve", 1))
                        idx = (p_ * 2 + s_) * NH + h
                        E("dve", lambda v, idx=idx: v.tensor_max(
                            nrm[0:1, idx:idx + 1], nrm[0:1, idx:idx + 1],
                            smt[0:1, 32:33]), inc=("dve", 1))
                        dstt = qfs if p_ == 0 else kfs
                        E("gp", lambda g, h=h, qb=qb, s_=s_, trow=trow,
                          dstt=dstt: g.dma_start(
                              out=dstt[(s_ * NH + h) * P:
                                       (s_ * NH + h + 1) * P, trow:trow + T],
                              in_=qkst[:, qb, :]),
                            waits=[("dve", ev_rope)], inc=(dsem, 16))
                        prev["qk_dma"][qb] = plan.c(dsem)
                    prev["lnb_pe"] = plan.c("pe")
                    prev["csb_pe"] = plan.c("pe")
                else:
                    for ts_ in range(4):
                        for dg in range(4):
                            i16 = ts_ * 4 + dg
                            bank = 4 + i16 % 2
                            mw = [(ressem[rb], ev_res[rb]), ev_ready,
                                  ("act", prev["vact"] if i16 >= 2 else 0)]
                            last = 0
                            for dc in range(DC):
                                last = E("pe", lambda t, ts_=ts_, dg=dg,
                                         dc=dc, bank=bank, rb=rb: t.matmul(
                                             pbank[bank][:],
                                             lnb[:, dc, ts_ * P:(ts_ + 1) * P],
                                             resv[rb][:, dc,
                                                      dg * T:(dg + 1) * T],
                                             start=(dc == 0),
                                             stop=(dc == DC - 1)),
                                         waits=(mw if dc == 0 else []),
                                         inc=("pe", 1))
                            ev_va = E("act", lambda a, i16=i16, bank=bank:
                                      a.activation(sqb[:, i16 % 2, :],
                                                   pbank[bank][:], AF.Copy),
                                      waits=[("pe", last),
                                             ("svt", prev["vdma"])],
                                      inc=("act", 1))
                            prev["vact"] = ev_va
                            E("gp", lambda g, ts_=ts_, dg=dg, s_=s_,
                              trow=trow, i16=i16: g.dma_start(
                                  out=vts[s_ * S + trow + ts_ * P:
                                          s_ * S + trow + (ts_ + 1) * P,
                                          dg * T:(dg + 1) * T],
                                  in_=sqb[:, i16 % 2, :]),
                                waits=[("act", ev_va)], inc=("svt", 16))
                            prev["vdma"] = plan.c("svt")
                    prev["lnb_pe"] = plan.c("pe")
            ev_res_pe[rb] = plan.c("pe")

        plan.barrier()

        E("dve", lambda v: v.tensor_mul(
            smt[0:1, 0:2 * NH], nrm[0:1, 0:2 * NH], nrm[0:1, 2 * NH:4 * NH]),
            inc=("dve", 1))
        ev_a2 = plan.c("dve")
        ev_sq_b = E("act", lambda a: a.activation(
            smt[0:1, 32:32 + 2 * NH], smt[0:1, 0:2 * NH], AF.Sqrt),
            waits=[("dve", ev_a2)], inc=("act", 1))
        ev_nb = E("dve", lambda v: v.tensor_scalar_mul(
            smt[0:1, 32:32 + 2 * NH], smt[0:1, 32:32 + 2 * NH], -1.0),
            waits=[("act", ev_sq_b)], inc=("dve", 1))
        ev_nbs = E("sy", lambda s: s.dma_start(
            out=miscd[2:3, 0:2 * NH], in_=smt[0:1, 32:32 + 2 * NH]),
            waits=[("dve", ev_nb)], inc=("snb", 16))
        E("sy", lambda s: s.dma_start(
            out=negB[:, :],
            in_=miscd[2:3, 0:2 * NH].broadcast_to((P, 2 * NH))),
            waits=[("snb", ev_nbs)], inc=("snb", 16))
        E("sy", lambda s: s.dma_start(out=mskb[:, :, :], in_=masks),
          inc=("kc", 16))
        ev_msk = plan.c("kc")
        ev_snb = plan.c("snb")

        if not attn_b:
            return
        ev_wo = {0: 0, 1: 0}
        ev_wope = {0: 0, 1: 0}
        pB = {"exp_act": {}, "pt_pe": {}, "af_dve": 0, "x_dve": 0,
              "lnb_pe2": 0, "s45_dve": 0, "cast_last": 0, "xa_gp": 0,
              "recip": 0, "linv_dve": 0, "stats_pe": 0, "evac_dve": 0,
              "qf_pe": 0, "kf_pe": {0: 0, 1: 0}, "vr_pe": 0}
        for s_ in range(NSEQ):
            E("gp", lambda g, s_=s_: g.dma_start(
                out=vrs[:, :, :],
                in_=vts[s_ * S:(s_ + 1) * S, :].rearrange(
                    "(c p) d -> p c d", p=P)),
                waits=[("pe", pB["vr_pe"])], inc=("svr", 16))
            ev_vr = plan.c("svr")
            for qt in range(TPS):
                trow = qt * T
                nkc = 4 * qt + 4
                for c in range(DC):
                    E("sy", lambda s, c=c, s_=s_, trow=trow: s.dma_start(
                        out=xa[:, c, :],
                        in_=x1f[s_ * DM + c * P:s_ * DM + (c + 1) * P,
                                trow:trow + T]),
                        waits=[("stx", pB["xa_gp"]), ("dve", pB["x_dve"]),
                               ("act", pB["cast_last"])], inc=("sx", 16))
                ev_x = plan.c("sx")
                for h in range(NH):
                    qb = h % 2
                    qsem = "qfa" if qb == 0 else "qfb"
                    ksem = "kfa" if qb == 0 else "kfb"
                    E("sy", lambda s, h=h, qb=qb, s_=s_, trow=trow:
                      s.dma_start(
                          out=qkst[:, qb, :],
                          in_=qfs[(s_ * NH + h) * P:(s_ * NH + h + 1) * P,
                                  trow:trow + T]),
                        waits=[("pe", pB["qf_pe"])], inc=(qsem, 16))
                    ev_qf = plan.c(qsem)
                    E("sy", lambda s, h=h, qb=qb, s_=s_, nkc=nkc:
                      s.dma_start(
                          out=kfv[:, qb, 0:nkc * P],
                          in_=kfs[(s_ * NH + h) * P:(s_ * NH + h + 1) * P,
                                  0:nkc * P]),
                        waits=[("pe", pB["kf_pe"][qb])], inc=(ksem, 16))
                    ev_kf = plan.c(ksem)
                    ev_l = 0
                    ev_o = 0
                    for kc in range(nkc):
                        bank = kc % 2
                        ev_st_ = E("pe", lambda t, kc=kc, qb=qb, bank=bank:
                                   t.matmul(pbank[bank][:],
                                            kfv[:, qb, kc * P:(kc + 1) * P],
                                            qkst[:, qb, :],
                                            start=True, stop=True),
                                   waits=[(ksem, ev_kf), (qsem, ev_qf),
                                          ("act",
                                           pB["exp_act"].get(bank, 0))],
                                   inc=("pe", 1))
                        idx = s_ * NH + h
                        ev_e = E("act", lambda a, kc=kc, bank=bank, idx=idx:
                                 a.activation(ptb[:, kc % 4, :],
                                              pbank[bank][:], AF.Exp,
                                              bias=negB[:, idx:idx + 1]),
                                 waits=[("pe", ev_st_), ("snb", ev_snb),
                                        ("kc", ev_msk),
                                        ("pe", pB["pt_pe"].get(kc % 4, 0))],
                                 inc=("act", 1))
                        pB["exp_act"][bank] = ev_e
                        ev_pt = ("act", ev_e)
                        if kc >= 4 * qt:
                            ev_mk = E("dve", lambda v, kc=kc, qt=qt:
                                      v.tensor_mul(ptb[:, kc % 4, :],
                                                   ptb[:, kc % 4, :],
                                                   mskb[:, kc - 4 * qt, :]),
                                      waits=[("act", ev_e)], inc=("dve", 1))
                            ev_pt = ("dve", ev_mk)
                        ev_l = E("pe", lambda t, kc=kc, nkc=nkc: t.matmul(
                            pbank[2][0:1, :], onesc[:], ptb[:, kc % 4, :],
                            start=(kc == 0), stop=(kc == nkc - 1)),
                            waits=[ev_pt,
                                   ("dve", pB["recip"] if kc == 0 else 0)],
                            inc=("pe", 1))
                        ev_o = E("pe", lambda t, kc=kc, h=h, nkc=nkc:
                                 t.matmul(pbank[3][:],
                                          vrs[:, kc, h * P:(h + 1) * P],
                                          ptb[:, kc % 4, :],
                                          start=(kc == 0),
                                          stop=(kc == nkc - 1)),
                                 waits=([("svr", ev_vr),
                                         ("dve", pB["af_dve"])]
                                        if kc == 0 else []),
                                 inc=("pe", 1))
                        pB["pt_pe"][kc % 4] = ev_o
                    E("dve", lambda v: v.reciprocal(
                        srow[0:1, 0, :], pbank[2][0:1, :]),
                        waits=[("pe", ev_l), ("sst", srow_free["ev"])],
                        inc=("dve", 1))
                    ev_r = E("dve", lambda v: v.tensor_copy(
                        linvb[:], srow[0:1, 0, :]), inc=("dve", 1))
                    pB["recip"] = ev_r
                    ev_lb = E("pe", lambda t: t.matmul(
                        pbank[4][:], onesr[:], linvb[:],
                        start=True, stop=True),
                        waits=[("dve", ev_r), ("dve", pB["linv_dve"]),
                               ("dve", pB["s45_dve"])], inc=("pe", 1))
                    ev_lc = E("dve", lambda v: v.tensor_copy(
                        linvr[:], pbank[4][:]),
                        waits=[("pe", ev_lb)], inc=("dve", 1))
                    pB["af_dve"] = E("dve", lambda v, h=h: v.tensor_mul(
                        afb[:, h, :], pbank[3][:], linvr[:]),
                        waits=[("pe", ev_o), ("dve", ev_lc)] +
                              ([("pe", pB["lnb_pe2"])] if h == 0 else []),
                        inc=("dve", 1))
                    pB["linv_dve"] = pB["af_dve"]
                    pB["qf_pe"] = plan.c("pe")
                    pB["kf_pe"][qb] = plan.c("pe")
                for half in range(2):
                    for h in range(NH):
                        wb = h % 2
                        wsem = "woa" if wb == 0 else "wob"
                        E("gp", lambda g, h=h, half=half, wb=wb: g.dma_start(
                            out=wov[:, wb, :],
                            in_=wot[h * P:(h + 1) * P,
                                    half * 1024:(half + 1) * 1024]),
                            waits=[("pe", ev_wope[wb])], inc=(wsem, 16))
                        ev_wo[wb] = plan.c(wsem)
                        last = 0
                        for dc8 in range(8):
                            last = E("pe", lambda t, h=h, dc8=dc8, wb=wb:
                                     t.matmul(
                                         pbank[dc8][:],
                                         wov[:, wb, dc8 * P:(dc8 + 1) * P],
                                         afb[:, h, :],
                                         start=(h == 0), stop=(h == NH - 1)),
                                     waits=([(wsem, ev_wo[wb]),
                                             ("dve", pB["af_dve"]),
                                             ("dve", pB["evac_dve"]),
                                             ("act", pB["exp_act"].get(
                                                 dc8 % 2, 0))]
                                            if dc8 == 0 else []),
                                     inc=("pe", 1))
                        ev_wope[wb] = last
                    ev_a = 0
                    for dc8 in range(8):
                        d = half * 8 + dc8
                        ev_a = E("dve", lambda v, d=d, dc8=dc8: v.tensor_add(
                            xa[:, d, :], pbank[dc8][:], xa[:, d, :]),
                            waits=[("pe", ev_wope[1]), ("sx", ev_x)],
                            inc=("dve", 1))
                        pB["x_dve"] = ev_a
                        E("gp", lambda g, d=d, s_=s_, trow=trow: g.dma_start(
                            out=x2f[s_ * DM + d * P:s_ * DM + (d + 1) * P,
                                    trow:trow + T],
                            in_=xa[:, d, :]),
                            waits=[("dve", ev_a)], inc=("stx", 16))
                    pB["evac_dve"] = plan.c("dve")
                if upto >= 3:
                    ev_s2 = 0
                    for d in range(DC):
                        cw = [("dve", pB["evac_dve"])]
                        if d == 0:
                            cw += [("dve", pB["s45_dve"]),
                                   ("pe", pB["stats_pe"]),
                                   ("dve", pB["linv_dve"])]
                        ev_c = E("act", lambda a, d=d: a.activation(
                            lnb[:, d, :], xa[:, d, :], AF.Copy),
                            waits=cw, inc=("act", 1))
                        pB["cast_last"] = ev_c
                        evq = E("dve", lambda v, d=d: v.tensor_mul(
                            sqb[:, d % 2, :], lnb[:, d, :], lnb[:, d, :]),
                            waits=[("act", ev_c),
                                   ("pe", ev_s2 if d >= 2 else 0)],
                            inc=("dve", 1))
                        E("pe", lambda t, d=d: t.matmul(
                            pbank[4][0:1, :], onesc[:], lnb[:, d, :],
                            start=(d == 0), stop=(d == DC - 1)),
                            waits=[("act", ev_c), ("dve", pB["linv_dve"])],
                            inc=("pe", 1))
                        ev_s2 = E("pe", lambda t, d=d: t.matmul(
                            pbank[5][0:1, :], onesc[:], sqb[:, d % 2, :],
                            start=(d == 0), stop=(d == DC - 1)),
                            waits=[("dve", evq)], inc=("pe", 1))
                    finish_stats(s_ * TPS + qt, 0, ev_s2)
                    pB["s45_dve"] = plan.c("dve")
                    pB["stats_pe"] = plan.c("pe")
                pB["xa_gp"] = plan.c("stx")
                pB["lnb_pe2"] = plan.c("pe")
            pB["vr_pe"] = plan.c("pe")

    ev_pro = 0
    ev_pro_pe = 0
    ev_pro_act = 0
    for ti in range(NT):
        s_, trow = ti // TPS, (ti % TPS) * T
        for c in range(DC):
            E("act", lambda s, c=c, s_=s_, trow=trow: s.dma_start(
                out=xa[:, 8 + c % 8, :],
                in_=x0f[s_ * DM + c * P:s_ * DM + (c + 1) * P,
                        trow:trow + T]),
                waits=[("act", ev_pro_act), ("sst", srow_free["ev"]),
                       ("dve", plan.c("dve") if ti == 0 and c == 0 else 0)],
                inc=("sx", 16)) if c < 8 else None
        ev_mm2 = 0
        ev_sqp = [0] * DC
        for half in range(2):
            hw = [("sx", plan.c("sx"))]
            for c in range(8):
                cc_ = half * 8 + c
                ev_castp = E("act", lambda a, c=c, cc_=cc_: a.activation(
                    qkst[:, cc_ % 2, :], xa[:, 8 + c, :], AF.Copy),
                    waits=hw + [("pe", ev_sqp[cc_ - 2] if cc_ >= 2 else
                                 (ev_pro_pe if ti else 0))],
                    inc=("act", 1))
                evq = E("dve", lambda v, cc_=cc_: v.tensor_mul(
                    sqb[:, cc_ % 2, :], qkst[:, cc_ % 2, :],
                    qkst[:, cc_ % 2, :]),
                    waits=[("act", ev_castp),
                           ("pe", ev_sqp[cc_ - 2] if cc_ >= 2 else
                            (ev_pro_pe if ti else 0))],
                    inc=("dve", 1))
                E("pe", lambda t, cc_=cc_: t.matmul(
                    pbank[4][0:1, :], onesc[:], qkst[:, cc_ % 2, :],
                    start=(cc_ == 0), stop=(cc_ == DC - 1)),
                    waits=[("act", ev_castp), ("dve", ev_ones)],
                    inc=("pe", 1))
                ev_sqp[cc_] = ev_mm2 = E("pe", lambda t, cc_=cc_: t.matmul(
                    pbank[5][0:1, :], onesc[:], sqb[:, cc_ % 2, :],
                    start=(cc_ == 0), stop=(cc_ == DC - 1)),
                    waits=[("dve", evq)], inc=("pe", 1))
            if half == 0:
                for c in range(8):
                    E("act", lambda s, c=c, s_=s_, trow=trow: s.dma_start(
                        out=xa[:, 8 + c, :],
                        in_=x0f[s_ * DM + (8 + c) * P:
                                s_ * DM + (9 + c) * P, trow:trow + T]),
                        waits=[("act", plan.c("act"))], inc=("sx", 16))
        ev_pro = finish_stats(ti, 0, ev_mm2)
        ev_pro_pe = plan.c("pe")
        ev_pro_act = plan.c("act")
    ev_c12_1 = sqrt_epoch(0, ev_pro)

    quantize(w1t, w1q, DC, FF, w1cb, 0, DM * FF, "p0a", "p0b", "t1s", "t1")
    ev_wq = quantize(w2t, w2q, FC, DM, w2cb, 1, DM * FF, "p0c", "p0d",
                     "t2s", "t2")

    plan.barrier()
    dst1 = yf if upto == 1 else x1f
    ffn_phase(x0f, dst1, 0, 1, do_stats=(upto > 1), ev_c12_ready=ev_c12_1)

    if upto >= 2:
        plan.barrier()
        ev_c12_2 = sqrt_epoch(1, plan.c("sst"))
        attn_phase(ev_c12_2)
        plan.barrier()
        if upto == 2:
            srcf = x2f if attn_b else x1f
            for c in range(2 * DC):
                E("sy", lambda s, c=c, srcf=srcf: s.dma_start(
                    out=yf[c * P:(c + 1) * P, :],
                    in_=srcf[c * P:(c + 1) * P, :]),
                    inc=("stx", 16))
        else:
            ev_c12_3 = sqrt_epoch(0, plan.c("sst"))
            ffn_phase(x2f, yf, 0, 1, do_stats=False, ev_c12_ready=ev_c12_3)

    plan.barrier()

    if debug:
        for name, ap in [("d_w1q", w1q), ("d_w2q", w2q), ("d_stats", statsd),
                         ("d_c12", c12d), ("d_misc", miscd), ("d_x1", x1f), ("d_x2", x2f),
                         ("d_ln2", ln2f), ("d_qf", qfs), ("d_kf", kfs),
                         ("d_vt", vts)]:
            o = nc.dram_tensor(name, list(ap.shape), ap.dtype,
                               kind="ExternalOutput").ap()
            n = ap.shape[0]
            for r0 in range(0, n, 2048):
                r1 = min(n, r0 + 2048)
                E("sy", lambda s, o=o, ap=ap, r0=r0, r1=r1: s.dma_start(
                    out=o[r0:r1, :], in_=ap[r0:r1, :]), inc=("sdbg", 16))
        o = nc.dram_tensor("d_nrm", [1, 4 * NH], F32,
                           kind="ExternalOutput").ap()
        E("sy", lambda s, o=o: s.dma_start(out=o, in_=nrm[0:1, :]),
          inc=("sdbg", 16))
        o2 = nc.dram_tensor("d_smt", [1, 64], F32, kind="ExternalOutput").ap()
        E("sy", lambda s, o2=o2: s.dma_start(out=o2, in_=smt[0:1, :]),
          inc=("sdbg", 16))
        o3 = nc.dram_tensor("d_negB", [2, 2 * NH], F32,
                            kind="ExternalOutput").ap()
        E("sy", lambda s, o3=o3: s.dma_start(out=o3, in_=negB[0:2, :]),
          inc=("sdbg", 16))
        plan.barrier()

    block = ctx.enter_context(nc.Block())
    for eng, attr in (("sy", "sync"), ("gp", "gpsimd"), ("pe", "tensor"),
                      ("dve", "vector"), ("act", "scalar")):
        def body(obj, eng=eng):
            for f in plan.thunks[eng]:
                f(obj)
        getattr(block, attr)(body)
    ctx.close()
    return nc


def host_inputs(L0_attn_x, L0_attn_Wq, L0_attn_Wk, L0_attn_Wv, L0_attn_Wo,
                L0_ffn_w1, L0_ffn_b1, L0_ffn_w2, L0_ffn_b2):
    x = np.asarray(L0_attn_x, np.float32)
    deint = np.concatenate(
        [np.concatenate([np.arange(h * 128, (h + 1) * 128, 2),
                         np.arange(h * 128 + 1, (h + 1) * 128, 2)])
         for h in range(NH)])
    com = {
        "w1t": np.ascontiguousarray(np.asarray(L0_ffn_w1, np.float32).T),
        "w2t": np.ascontiguousarray(np.asarray(L0_ffn_w2, np.float32).T),
        "b1b": np.ascontiguousarray(
            np.asarray(L0_ffn_b1, np.float32).reshape(FC, P).T),
        "b2b": np.ascontiguousarray(
            np.asarray(L0_ffn_b2, np.float32).reshape(DC, P).T),
        "wqt": np.ascontiguousarray(np.asarray(L0_attn_Wq, np.float32)[deint].T),
        "wkt": np.ascontiguousarray(np.asarray(L0_attn_Wk, np.float32)[deint].T),
        "wvt": np.ascontiguousarray(np.asarray(L0_attn_Wv, np.float32).T),
        "wot": np.ascontiguousarray(np.asarray(L0_attn_Wo, np.float32).T),
    }
    invf = 1.0 / (10000.0 ** (np.arange(0, 128, 2, dtype=np.float32)
                              / np.float32(128.0)))
    fr = np.outer(invf.astype(np.float32), np.arange(S, dtype=np.float32))
    cos = np.cos(fr).astype(np.float32)
    sin = np.sin(fr).astype(np.float32)
    sc = np.float32(1.0 / np.sqrt(np.float32(128.0)))
    com["cost"] = np.ascontiguousarray(np.stack([cos, sin], 1))
    com["costq"] = np.ascontiguousarray(np.stack([cos * sc, sin * sc], 1))
    kk = np.arange(P)[:, None, None]
    jj = np.arange(4)[None, :, None]
    qq = np.arange(T)[None, None, :]
    com["masks"] = ((jj * P + kk) <= qq).astype(ml_dtypes.bfloat16)
    in_maps = []
    for c in range(NCORES):
        xc = np.ascontiguousarray(
            x[2 * c:2 * c + 2].transpose(0, 2, 1)).reshape(NSEQ * DM, S)
        in_maps.append(dict(com, x0f=xc))
    return in_maps


def kernel(**inputs):
    in_maps = host_inputs(**inputs)
    nc = build_program(UPTO, DEBUG, ATTN_B, ATTN_NP)
    res = run_bass_kernel_spmd(nc, in_maps, core_ids=list(range(NCORES)))
    out = np.empty((16, S, DM), np.float32)
    for c in range(NCORES):
        yfc = res.results[c]["yf"].reshape(NSEQ, DM, S)
        out[2 * c] = yfc[0].T
        out[2 * c + 1] = yfc[1].T
    if DEBUG:
        kernel.dbg = res.results
    return out


# version 03 (final)
# speedup vs baseline: 1.0941x; 1.0941x over previous
import numpy as np
import ml_dtypes
import concourse.bass as bass
import concourse.mybir as mybir
from concourse.bass_utils import run_bass_kernel_spmd

F32 = mybir.dt.float32
BF16 = mybir.dt.bfloat16
AF = mybir.ActivationFunctionType
ALU = mybir.AluOpType
AX = mybir.AxisListType

P = 128
T = 512
DM = 2048
DC = DM // P
FF = 8192
FC = FF // P
NH = 16
S = 2048
NSEQ = 2
TPS = S // T
NT = NSEQ * TPS
NCORES = 8
EPS_LN = 1e-5

UPTO = 3
ATTN_B = True
ATTN_NP = 3
DEBUG = False

DMA_SEMS = ("sx", "scs", "w1a", "w1b", "w2a", "w2b", "stx", "sst", "sc12",
            "cc", "sr", "p0a", "p0b", "p0c", "p0d", "kc", "t1s", "t1",
            "t2s", "t2", "sqw", "wra", "wrb", "sln", "sqf", "skf", "svt",
            "svr", "woa", "wob", "qfa", "qfb", "kfa", "kfb", "snb", "sdbg")


class Plan:
    ENGINES = ("pe", "dve", "act", "sy", "gp")

    def __init__(self):
        self.thunks = {e: [] for e in self.ENGINES}
        self.cnt = {}
        self.last_wait = {}
        self.sems = {}

    def emit(self, eng, fn, waits=(), inc=None):
        wl = []
        for s, v in waits:
            if not v or v <= 0:
                continue
            k = (eng, s)
            if self.last_wait.get(k, 0) >= v:
                continue
            self.last_wait[k] = v
            wl.append((s, v))
        after = None
        if inc:
            sem, n = inc
            self.cnt[sem] = self.cnt.get(sem, 0) + n
            after = self.cnt[sem]
        sems = self.sems

        def thunk(obj, wl=wl, fn=fn, inc=inc):
            for s, v in wl:
                obj.wait_ge(sems[s], v)
            ins = fn(obj)
            if inc:
                ins.then_inc(sems[inc[0]], inc[1])
        self.thunks[eng].append(thunk)
        return after

    def c(self, sem):
        return self.cnt.get(sem, 0)

    def barrier(self):
        snap = {s: v for s, v in self.cnt.items() if v > 0}
        for e in self.ENGINES:
            waits = [(s, v) for s, v in snap.items() if s != e]
            self.emit(e, lambda obj: obj.nop(), waits=waits)


def build_program(upto, debug=False, attn_b=True, attn_np=3):
    nc = bass.Bass("TRN2", target_bir_lowering=False, debug=False)
    plan = Plan()
    E = plan.emit

    def din(name, shape, dtype=F32):
        return nc.dram_tensor(name, shape, dtype, kind="ExternalInput").ap()

    x0f = din("x0f", [NSEQ * DM, S])
    w1t = din("w1t", [DM, FF])
    w2t = din("w2t", [FF, DM])
    b1b = din("b1b", [P, FC])
    b2b = din("b2b", [P, DC])
    wqt = din("wqt", [DM, DM])
    wkt = din("wkt", [DM, DM])
    wvt = din("wvt", [DM, DM])
    wot = din("wot", [DM, DM])
    cost = din("cost", [64, 2, S])
    costq = din("costq", [64, 2, S])
    masks = din("masks", [P, 4, T], BF16)

    yf = nc.dram_tensor("yf", [NSEQ * DM, S], F32, kind="ExternalOutput").ap()

    def dsc(n, sh, dt=F32):
        return nc.dram_tensor(n, sh, dt, kind="Internal").ap()

    x1f = dsc("x1f", [NSEQ * DM, S])
    x2f = dsc("x2f", [NSEQ * DM, S])
    w1q = dsc("w1q", [DM, FF], BF16)
    w2q = dsc("w2q", [FF, DM], BF16)
    ln2f = dsc("ln2f", [NSEQ * DM, S], BF16)
    qfs = dsc("qfs", [NSEQ * NH * P, S], BF16)
    kfs = dsc("kfs", [NSEQ * NH * P, S], BF16)
    vts = dsc("vts", [NSEQ * S, DM], BF16)
    statsd = dsc("statsd", [6, NT * T])
    c12d = dsc("c12d", [6, NT * T])
    miscd = dsc("miscd", [4, 64])

    from contextlib import ExitStack
    ctx = ExitStack()
    sb = lambda n, sh, dt: ctx.enter_context(nc.sbuf_tensor(n, sh, dt))

    xa = sb("xa", [P, DC, T], F32)
    lnb = sb("lnb", [P, DC, T], BF16)
    hst = sb("hst", [P, FC * T], BF16)
    wstr = sb("wstr", [P, FC * T], BF16)
    t1b = sb("t1b", [P, T], F32)
    t2b = sb("t2b", [P, 2, T], F32)
    sqb = sb("sqb", [P, 2, T], BF16)
    c1r = sb("c1r", [P, T], F32)
    c2r = sb("c2r", [P, T], F32)
    onesc = sb("onesc", [P, 1], BF16)
    onescf = sb("onescf", [P, 1], F32)
    onesr = sb("onesr", [1, P], F32)
    epsc = sb("epsc", [P, 1], F32)
    b1c = sb("b1c", [P, FC], F32)
    b2h = sb("b2h", [P, DC], F32)
    thrc = sb("thrc", [P, 2], F32)
    prt = sb("prt", [P, FC + 1], F32)
    smt = sb("smt", [1, 64], F32)
    srow = sb("srow", [1, 2, T], F32)
    crow = sb("crow", [1, 2, T], F32)
    rta = sb("rta", [64, T], F32)
    rtb = sb("rtb", [64, T], F32)
    qkst = sb("qkst", [P, 2, T], BF16)
    nrm = sb("nrm", [1, 4 * NH], F32)
    negB = sb("negB", [P, 2 * NH], F32)
    linvb = sb("linvb", [1, T], F32)
    linvr = sb("linvr", [P, T], F32)

    csb = t2b.rearrange("p a t -> p (a t)").bitcast(F32).rearrange(
        "p (a t) -> p a t", a=2)[0:64, :, :]
    qnsq = sqb[:, 0, :]
    afb = lnb
    ptb = t2b.rearrange("p a t -> p (a t)").bitcast(BF16).rearrange(
        "p (a t) -> p a t", a=4)
    wov = wstr[:, 0:2 * 1024].rearrange("p (a n) -> p a n", a=2)
    kfv = wstr[:, 2 * 1024:2 * 1024 + 2 * S].rearrange("p (a n) -> p a n", a=2)
    _o = 2 * 1024 + 2 * S
    mskb = wstr[:, _o:_o + 4 * T].rearrange("p (a t) -> p a t", a=4)

    w1cb = hst.bitcast(F32).rearrange("p (a f) -> p a f", a=2)
    awb = wstr[:, 0:2 * FF].bitcast(F32)
    qcb = wstr[:, 2 * FF:4 * FF].rearrange("p (a f) -> p a f", a=2)
    sgb = lnb.rearrange("p c t -> p (c t)")
    w2cb = xa.rearrange("p c t -> p (c t)")[:, 0:4096].rearrange(
        "p (a f) -> p a f", a=2)

    w1v = wstr[:, 0:2 * DC * T].rearrange("p (a c f) -> p a c f", a=2, c=DC)
    w2v = wstr[:, 2 * DC * T:4 * DC * T].rearrange(
        "p (a c j) -> p a c j", a=2, c=FC)
    hsv = hst.rearrange("p (c t) -> p c t", c=FC)
    wres = hst.rearrange("p (c n) -> p c n", c=DC)
    wpre = wstr.rearrange("p (c n) -> p c n", c=DC)
    vrs = hst.rearrange("p (c n) -> p c n", c=DC)

    pbank = [ctx.enter_context(nc.psum_tensor(f"pb{i}", [P, T], F32))
             for i in range(8)]
    for e in Plan.ENGINES:
        plan.sems[e] = ctx.enter_context(nc.semaphore(f"sem_{e}"))
    for s in DMA_SEMS:
        plan.sems[s] = ctx.enter_context(nc.semaphore(f"sem_{s}"))

    ev_ones = E("dve", lambda v: v.memset(onesc[:], 1.0), inc=("dve", 1))
    E("dve", lambda v: v.memset(onescf[:], 1.0), inc=("dve", 1))
    E("dve", lambda v: v.memset(onesr[:], 1.0), inc=("dve", 1))
    E("dve", lambda v: v.memset(epsc[:], EPS_LN), inc=("dve", 1))
    E("dve", lambda v: v.memset(nrm[:], 0.0), inc=("dve", 1))
    E("sy", lambda s: s.dma_start(out=b1c[:], in_=b1b), inc=("kc", 16))
    E("sy", lambda s: s.dma_start(out=b2h[:], in_=b2b), inc=("kc", 16))
    ev_kc = plan.c("kc")
    ev_b2h = E("dve", lambda v: v.tensor_scalar_mul(b2h[:], b2h[:], 0.5),
               waits=[("kc", ev_kc)], inc=("dve", 1))

    def quantize(wt, wq, nchunk, width, cbuf, thr_i, nelem, sema, semb,
                 tsem_s, tsem):
        nseg = width // 2048
        ev_dma = [0] * nchunk
        ev_red = [0] * nchunk
        for c in range(nchunk):
            b = c % 2
            ssem = sema if b == 0 else semb
            w = [("dve", ev_red[c - 2])] if c >= 2 else []
            E("sy", lambda s, c=c, b=b: s.dma_start(
                out=cbuf[:, b, 0:width], in_=wt[c * P:(c + 1) * P, :]),
                waits=w, inc=(ssem, 16))
            ev_dma[c] = plan.c(ssem)
            for g in range(nseg):
                ev_red[c] = E("dve", lambda v, c=c, b=b, g=g: v.tensor_reduce(
                    prt[:, c * nseg + g:c * nseg + g + 1],
                    cbuf[:, b, g * 2048:(g + 1) * 2048], AX.X, ALU.add,
                    apply_absolute_value=True),
                    waits=[(ssem, ev_dma[c])], inc=("dve", 1))
        ev_p = E("dve", lambda v: v.tensor_reduce(
            prt[:, FC:FC + 1], prt[:, 0:nchunk * nseg], AX.X, ALU.add),
            inc=("dve", 1))
        ev_tot = E("pe", lambda t: t.matmul(
            pbank[7][0:1, 0:1], onescf[:], prt[:, FC:FC + 1],
            start=True, stop=True),
            waits=[("dve", ev_p)], inc=("pe", 1))
        ev_thr = E("dve", lambda v: v.tensor_scalar(
            smt[0:1, 0:1], pbank[7][0:1, 0:1], 0.5 / nelem, 0.5e-8,
            ALU.mult, ALU.max),
            waits=[("pe", ev_tot)], inc=("dve", 1))
        ev_out = E("sy", lambda s: s.dma_start(
            out=miscd[thr_i:thr_i + 1, 0:1], in_=smt[0:1, 0:1]),
            waits=[("dve", ev_thr)], inc=(tsem_s, 16))
        ev_bc = E("sy", lambda s: s.dma_start(
            out=thrc[:, thr_i:thr_i + 1],
            in_=miscd[thr_i:thr_i + 1, 0:1].broadcast_to((P, 1))),
            waits=[(tsem_s, ev_out)], inc=(tsem, 16))
        ev_ge = [0] * nchunk
        ev_q = [0] * nchunk
        ev_sg = [0] * nchunk
        for c in range(nchunk):
            b = c % 2
            ssem = sema if b == 0 else semb
            w = [(tsem, ev_bc)]
            if c >= 2:
                w.append(("dve", ev_q[c - 2]))
            else:
                w.append(("dve", ev_red[nchunk - 1]))
            E("sy", lambda s, c=c, b=b: s.dma_start(
                out=cbuf[:, b, 0:width], in_=wt[c * P:(c + 1) * P, :]),
                waits=w, inc=(ssem, 16))
            ev_dmab = plan.c(ssem)
            aw = [(ssem, ev_dmab)]
            if c >= 1:
                aw.append(("dve", ev_ge[c - 1]))
            ev_abs = E("act", lambda a, b=b: a.activation(
                awb[:, 0:width], cbuf[:, b, 0:width], AF.Abs),
                waits=aw, inc=("act", 1))
            sw = [("dve", ev_q[c - 1])] if c >= 1 else []
            ev_sg[c] = E("act", lambda a, b=b: a.activation(
                sgb[:, 0:width], cbuf[:, b, 0:width], AF.Sign),
                waits=sw, inc=("act", 1))
            ev_ge[c] = E("dve", lambda v, b=b, thr_i=thr_i: v.tensor_scalar(
                qcb[:, b, 0:width], awb[:, 0:width], thrc[:, thr_i:thr_i + 1],
                None, ALU.is_ge),
                waits=[("act", ev_abs)], inc=("dve", 1))
            ev_q[c] = E("dve", lambda v, b=b: v.tensor_mul(
                qcb[:, b, 0:width], qcb[:, b, 0:width], sgb[:, 0:width]),
                waits=[("act", ev_sg[c])], inc=("dve", 1))
            E("gp", lambda g, c=c, b=b: g.dma_start(
                out=wq[c * P:(c + 1) * P, :], in_=qcb[:, b, 0:width]),
                waits=[("dve", ev_q[c])], inc=("sqw", 16))
        return plan.c("sqw")


    srow_free = {"ev": 0}

    def finish_stats(ti, ln_idx, ev_mm2, extra=()):
        E("dve", lambda v: v.tensor_scalar_mul(
            srow[0:1, 1, :], pbank[4][0:1, :], 1.0 / DM),
            waits=[("pe", ev_mm2), ("sst", srow_free["ev"])] + list(extra),
            inc=("dve", 1))
        E("dve", lambda v: v.tensor_scalar_mul(
            srow[0:1, 0, :], pbank[5][0:1, :], 1.0 / DM), inc=("dve", 1))
        E("dve", lambda v: v.scalar_tensor_tensor(
            t1b[0:1, 0:T], srow[0:1, 1, :], -1.0, srow[0:1, 1, :],
            ALU.mult, ALU.mult), inc=("dve", 1))
        ev_v2 = E("dve", lambda v: v.tensor_add(
            srow[0:1, 0, :], srow[0:1, 0, :], t1b[0:1, 0:T]), inc=("dve", 1))
        sl = slice(ti * T, (ti + 1) * T)
        ev_o = E("sy", lambda s, sl=sl: s.dma_start(
            out=statsd[2 * ln_idx:2 * ln_idx + 2, sl], in_=srow[0:1, :, :]),
            waits=[("dve", ev_v2)], inc=("sst", 16))
        srow_free["ev"] = ev_o
        return ev_o

    def ln_stats_tile(src, ti, ln_idx, xw=(), kw=0):
        s_, trow = ti // TPS, (ti % TPS) * T
        for c in range(DC):
            E("sy", lambda s, c=c, s_=s_, trow=trow: s.dma_start(
                out=xa[:, c, :],
                in_=src[s_ * DM + c * P:s_ * DM + (c + 1) * P, trow:trow + T]),
                waits=list(xw), inc=("sx", 16))
        ev_x = plan.c("sx")
        ev_mm2 = 0
        ev_sq = [0] * DC
        for c in range(DC):
            ev_cast = E("act", lambda a, c=c: a.activation(
                lnb[:, c, :], xa[:, c, :], AF.Copy),
                waits=[("sx", ev_x), ("pe", kw)], inc=("act", 1))
            evq = E("dve", lambda v, c=c: v.tensor_mul(
                sqb[:, c % 2, :], lnb[:, c, :], lnb[:, c, :]),
                waits=[("act", ev_cast),
                       ("pe", ev_sq[c - 2] if c >= 2 else kw)],
                inc=("dve", 1))
            E("pe", lambda t, c=c: t.matmul(
                pbank[4][0:1, :], onesc[:], lnb[:, c, :],
                start=(c == 0), stop=(c == DC - 1)),
                waits=[("act", ev_cast), ("dve", ev_ones)], inc=("pe", 1))
            ev_sq[c] = ev_mm2 = E("pe", lambda t, c=c: t.matmul(
                pbank[5][0:1, :], onesc[:], sqb[:, c % 2, :],
                start=(c == 0), stop=(c == DC - 1)),
                waits=[("dve", evq)], inc=("pe", 1))
        return finish_stats(ti, ln_idx, ev_mm2)

    def sqrt_epoch(ln_idx, ready):
        ev_last = 0
        for ti in range(NT):
            sl = slice(ti * T, (ti + 1) * T)
            w = [("sst", ready), ("sc12", ev_last),
                 ("dve", plan.c("dve")), ("act", plan.c("act"))]
            E("sy", lambda s, sl=sl: s.dma_start(
                out=srow[0:1, :, :],
                in_=statsd[2 * ln_idx:2 * ln_idx + 2, sl]),
                waits=w, inc=("sr", 16))
            ev_s = E("act", lambda a: a.activation(
                crow[0:1, 0, :], srow[0:1, 0, :], AF.Sqrt, bias=epsc[0:1, :]),
                waits=[("sr", plan.c("sr"))], inc=("act", 1))
            E("dve", lambda v: v.reciprocal(linvb[:], crow[0:1, 0, :]),
              waits=[("act", ev_s)], inc=("dve", 1))
            E("dve", lambda v: v.tensor_copy(crow[0:1, 0, :], linvb[:]),
              inc=("dve", 1))
            ev_c2 = E("dve", lambda v: v.scalar_tensor_tensor(
                crow[0:1, 1, :], srow[0:1, 1, :], -1.0, crow[0:1, 0, :],
                ALU.mult, ALU.mult), inc=("dve", 1))
            ev_last = E("sy", lambda s, sl=sl: s.dma_start(
                out=c12d[2 * ln_idx:2 * ln_idx + 2, sl], in_=crow[0:1, :, :]),
                waits=[("dve", ev_c2)], inc=("sc12", 16))
        return ev_last

    def ln_bcast(ln_idx, ti, ready, warw=(), eng="act"):
        sl = slice(ti * T, (ti + 1) * T)
        E(eng, lambda g, sl=sl: g.dma_start(
            out=c1r[:],
            in_=c12d[2 * ln_idx:2 * ln_idx + 1, sl].broadcast_to((P, T))),
            waits=[("sc12", ready)] + list(warw), inc=("cc", 16))
        E(eng, lambda g, sl=sl: g.dma_start(
            out=c2r[:],
            in_=c12d[2 * ln_idx + 1:2 * ln_idx + 2, sl].broadcast_to((P, T))),
            inc=("cc", 16))
        return plan.c("cc")

    def ln_tile(ev_x, ev_bc, lnw=()):
        evs = [0] * DC
        for c in range(DC):
            w = [("cc", ev_bc), ("sx", ev_x)] + (list(lnw) if c == 0 else [])
            E("dve", lambda v, c=c: v.tensor_mul(
                t1b[:, 0:T], xa[:, c, :], c1r[:]), waits=w, inc=("dve", 1))
            evs[c] = E("dve", lambda v, c=c: v.tensor_add(
                lnb[:, c, :], t1b[:, 0:T], c2r[:]), inc=("dve", 1))
        return evs

    def ffn_phase(src, dst, ln_idx, dst_ln_idx, do_stats, ev_c12_ready):
        ev_gel = {}
        ev_w1 = {0: 0, 1: 0}
        ev_w1pe = {0: 0, 1: 0}
        ev_w2 = {0: 0, 1: 0}
        ev_w2pe = {0: 0, 1: 0}
        ev_add = [0] * DC
        ev_cast = [0] * DC
        ev_sqv = [0] * DC
        ev_s2v = [0] * DC
        prev = {"xa_gp": 0, "xa_act": 0, "lnb_pe": 0, "bc_dve": 0,
                "s45_dve": 0}

        for ti in range(NT):
            s_, trow = ti // TPS, (ti % TPS) * T
            for c in range(DC):
                E("act", lambda s, c=c, s_=s_, trow=trow: s.dma_start(
                    out=xa[:, c, :],
                    in_=src[s_ * DM + c * P:s_ * DM + (c + 1) * P,
                            trow:trow + T]),
                    waits=[("stx", prev["xa_gp"]), ("act", prev["xa_act"]),
                           ("dve", prev["bc_dve"])], inc=("sx", 16))
            ev_x = plan.c("sx")
            ev_bc = ln_bcast(ln_idx, ti, ev_c12_ready,
                             warw=[("dve", prev["bc_dve"])], eng="act")
            ev_ln = ln_tile(ev_x, ev_bc,
                            lnw=[("pe", prev["lnb_pe"]),
                                 ("act", prev["xa_act"])])
            prev["bc_dve"] = ev_ln[-1]

            for g in range(DC):
                sl_ = g % 2
                wsem = "w1a" if sl_ == 0 else "w1b"
                E("sy", lambda s, g=g, sl_=sl_: s.dma_start(
                    out=w1v[:, sl_, :, :],
                    in_=w1q[:, g * T:(g + 1) * T].rearrange(
                        "(c p) f -> p c f", p=P)),
                    waits=[("pe", ev_w1pe[sl_]), ("sqw", ev_wq)],
                    inc=(wsem, 16))
                ev_w1[sl_] = plan.c(wsem)
                for f in range(4):
                    ffc = g * 4 + f
                    bank = (0, 1, 6, 7)[ffc % 4]
                    first = (ffc == 0)
                    mw = [(wsem, ev_w1[sl_]),
                          ("dve", ev_ln[0] if first else ev_ln[DC - 1]),
                          ("act", ev_gel.get(bank, 0))]
                    last = 0
                    for dc in range(DC):
                        dw = (mw if dc == 0 else
                              ([("dve", ev_ln[dc])] if first else []))
                        last = E("pe", lambda t, f=f, dc=dc, bank=bank,
                                 sl_=sl_: t.matmul(
                                     pbank[bank][:],
                                     w1v[:, sl_, dc, f * P:(f + 1) * P],
                                     lnb[:, dc, :],
                                     start=(dc == 0), stop=(dc == DC - 1)),
                                 waits=dw, inc=("pe", 1))
                    ev_w1pe[sl_] = last
                    ev_gel[bank] = E("act", lambda a, ffc=ffc, bank=bank:
                                     a.activation(hsv[:, ffc, :],
                                                  pbank[bank][:], AF.Gelu,
                                                  bias=b1c[:, ffc:ffc + 1]),
                                     waits=[("pe", last), ("kc", ev_kc)],
                                     inc=("act", 1))
            ev_gel_all = max(ev_gel[0], ev_gel[1])

            ev_s2 = 0
            for d in range(DC):
                sl_ = d % 2
                wsem = "w2a" if sl_ == 0 else "w2b"
                for q4 in range(4):
                    qeng = "gp" if q4 < 2 else "sy"
                    E(qeng, lambda g_, d=d, q4=q4, sl_=sl_: g_.dma_start(
                        out=w2v[:, sl_, q4 * DC:(q4 + 1) * DC, :],
                        in_=w2q[q4 * (FF // 4):(q4 + 1) * (FF // 4),
                                d * P:(d + 1) * P].rearrange(
                            "(c p) j -> p c j", p=P)),
                        waits=([("pe", ev_w2pe[sl_]), ("sqw", ev_wq)]
                               if q4 in (0, 2) else []),
                        inc=(wsem, 16))
                ev_w2[sl_] = plan.c(wsem)
                bank = 2 + sl_
                mw = [(wsem, ev_w2[sl_]), ("act", ev_gel_all),
                      ("dve", ev_add[(d - 2) % DC] if (ti or d >= 2) else 0)]
                last = 0
                for fc in range(FC):
                    last = E("pe", lambda t, fc=fc, bank=bank, sl_=sl_:
                             t.matmul(pbank[bank][:], w2v[:, sl_, fc, :],
                                      hsv[:, fc, :],
                                      start=(fc == 0), stop=(fc == FC - 1)),
                             waits=(mw if fc == 0 else []), inc=("pe", 1))
                ev_w2pe[sl_] = last
                E("dve", lambda v, d=d: v.tensor_scalar_add(
                    xa[:, d, :], xa[:, d, :], b2h[:, d:d + 1]),
                    waits=[("dve", ev_b2h)], inc=("dve", 1))
                ev_add[d] = E("dve", lambda v, d=d, bank=bank:
                              v.scalar_tensor_tensor(
                                  xa[:, d, :], pbank[bank][:], 0.5,
                                  xa[:, d, :], ALU.mult, ALU.add),
                              waits=[("pe", last)], inc=("dve", 1))
                E("gp", lambda g_, d=d, s_=s_, trow=trow: g_.dma_start(
                    out=dst[s_ * DM + d * P:s_ * DM + (d + 1) * P,
                            trow:trow + T],
                    in_=xa[:, d, :]),
                    waits=[("dve", ev_add[d])], inc=("stx", 16))
                if do_stats:
                    cw = [("dve", ev_add[d])]
                    if d == 0:
                        cw += [("dve", prev["s45_dve"]), ("pe", prev["lnb_pe"])]
                    ev_cast[d] = E("act", lambda a, d=d: a.activation(
                        lnb[:, d, :], xa[:, d, :], AF.Copy),
                        waits=cw, inc=("act", 1))
                    ev_sqv[d] = E("dve", lambda v, d=d: v.tensor_mul(
                        sqb[:, d % 2, :], lnb[:, d, :], lnb[:, d, :]),
                        waits=[("act", ev_cast[d]),
                               ("pe", ev_s2v[d - 2] if d >= 2 else 0)],
                        inc=("dve", 1))
                    for dd in ([d - 1] if d else []) + ([d] if d == DC - 1 else []):
                        E("pe", lambda t, dd=dd: t.matmul(
                            pbank[4][0:1, :], onesc[:], lnb[:, dd, :],
                            start=(dd == 0), stop=(dd == DC - 1)),
                            waits=[("act", ev_cast[dd])], inc=("pe", 1))
                        ev_s2 = E("pe", lambda t, dd=dd: t.matmul(
                            pbank[5][0:1, :], onesc[:], sqb[:, dd % 2, :],
                            start=(dd == 0), stop=(dd == DC - 1)),
                            waits=[("dve", ev_sqv[dd])], inc=("pe", 1))
                        ev_s2v[dd] = ev_s2
            if do_stats:
                finish_stats(ti, dst_ln_idx, ev_s2)
                prev["s45_dve"] = plan.c("dve")
            prev["xa_gp"] = plan.c("stx")
            prev["xa_act"] = ev_cast[DC - 1] if do_stats else 0
            prev["lnb_pe"] = plan.c("pe")

    def attn_phase(ev_c12_ready):
        resv = [wres, wpre]
        ressem = ["wra", "wrb"]
        ev_res = [0, 0]
        ev_res_pe = [0, 0]
        wts = [wqt, wkt, wvt]
        prev = {"lnb_pe": 0, "bc_dve": 0, "qk_dma": {0: 0, 1: 0},
                "qk_dve": {0: 0, 1: 0}, "csb_pe": 0, "vdma": 0, "vact": 0,
                "nrm_dve": 0}
        ev_lnst = [0] * NT
        ev_sqn = {0: 0, 1: 0}
        ev_nrm_mm = {0: 0, 1: 0}
        ev_nrm_rd = {0: 0, 1: 0}

        def emit_nrm(idx, jb):
            ev_nrm_mm[jb] = E("pe", lambda t, jb=jb: t.matmul(
                pbank[5][32 * jb:32 * jb + 1, :], onesc[:], sqb[:, jb, :],
                start=True, stop=True),
                waits=[("act", ev_sqn[jb]), ("dve", ev_nrm_rd[jb])],
                inc=("pe", 1))
            ev_nrm_rd[jb] = E("dve", lambda v, jb=jb: v.tensor_reduce(
                smt[0:1, 32:33], pbank[5][32 * jb:32 * jb + 1, :], AX.X,
                ALU.max),
                waits=[("pe", ev_nrm_mm[jb])], inc=("dve", 1))
            E("dve", lambda v, idx=idx: v.tensor_max(
                nrm[0:1, idx:idx + 1], nrm[0:1, idx:idx + 1],
                smt[0:1, 32:33]), inc=("dve", 1))

        for p_ in range(attn_np):
            rb = p_ % 2
            prev["qk_dma"] = {0: 0, 1: 0}
            rw = [("pe", ev_res_pe[rb])]
            if p_ == 0:
                rw += [("act", plan.c("act")), ("dve", plan.c("dve")),
                       ("pe", plan.c("pe")), ("sqw", ev_wq)]
            E("gp", lambda g, p_=p_, rb=rb: g.dma_start(
                out=resv[rb][:, :, :],
                in_=wts[p_].rearrange("(c p) n -> p c n", p=P)),
                waits=rw, inc=(ressem[rb], 16))
            ev_res[rb] = plan.c(ressem[rb])
            for ti in range(NT):
                s_, trow = ti // TPS, (ti % TPS) * T
                if p_ == 0:
                    for c in range(DC):
                        E("sy", lambda s, c=c, s_=s_, trow=trow: s.dma_start(
                            out=xa[:, c, :],
                            in_=x1f[s_ * DM + c * P:s_ * DM + (c + 1) * P,
                                    trow:trow + T]),
                            waits=[("dve", prev["bc_dve"]),
                                   ("stx", plan.c("stx"))], inc=("sx", 16))
                    ev_x = plan.c("sx")
                    ev_bc = ln_bcast(1, ti, ev_c12_ready,
                                     warw=[("dve", prev["bc_dve"])])
                    ev_ln = ln_tile(ev_x, ev_bc,
                                    lnw=[("pe", prev["lnb_pe"]),
                                         ("sln", ev_lnst[ti - 1]
                                          if ti else 0)])
                    prev["bc_dve"] = ev_ln[-1]
                    ev_ready = ("dve", ev_ln[DC - 1])
                    E("gp", lambda g, s_=s_, trow=trow: g.dma_start(
                        out=ln2f[s_ * DM:(s_ + 1) * DM,
                                 trow:trow + T].rearrange(
                            "(c p) t -> p c t", p=P),
                        in_=lnb[:, :, :]),
                        waits=[("dve", ev_ln[DC - 1])], inc=("sln", 16))
                    ev_lnst[ti] = plan.c("sln")
                else:
                    E("sy", lambda s, s_=s_, trow=trow: s.dma_start(
                        out=lnb[:, :, :],
                        in_=ln2f[s_ * DM:(s_ + 1) * DM,
                                 trow:trow + T].rearrange(
                            "(c p) t -> p c t", p=P)),
                        waits=[("sln", plan.c("sln")),
                               ("pe", prev["lnb_pe"])], inc=("sx", 16))
                    ev_ready = ("sx", plan.c("sx"))
                if p_ < 2:
                    tab = costq if p_ == 0 else cost
                    E("sy", lambda s, tab=tab, trow=trow: s.dma_start(
                        out=csb[:, :, :], in_=tab[:, :, trow:trow + T]),
                        waits=[("pe", prev["csb_pe"]),
                               ("dve", plan.c("dve") if ti == 0 and p_ == 0
                                else 0)],
                        inc=("scs", 16))
                    ev_csb = plan.c("scs")

                    for h in range(NH):
                        qb = h % 2
                        bank = 6 + qb
                        mw = [(ressem[rb], ev_res[rb]),
                              ("dve", prev["qk_dve"][qb])]
                        if p_ == 0 and h == 0:
                            mw.append(("dve", ev_ln[0]))
                        else:
                            mw.append(ev_ready)
                        last = 0
                        for dc in range(DC):
                            dw = (mw if dc == 0 else
                                  ([("dve", ev_ln[dc])]
                                   if p_ == 0 and h == 0 else []))
                            last = E("pe", lambda t, h=h, dc=dc, bank=bank,
                                     rb=rb: t.matmul(
                                         pbank[bank][:],
                                         resv[rb][:, dc, h * P:(h + 1) * P],
                                         lnb[:, dc, :],
                                         start=(dc == 0), stop=(dc == DC - 1)),
                                     waits=dw,
                                     inc=("pe", 1))
                        dsem = "sqf" if p_ == 0 else "skf"
                        rw2 = [("pe", last), ("scs", ev_csb),
                               (dsem, prev["qk_dma"][qb])]
                        E("dve", lambda v, bank=bank: v.tensor_mul(
                            rta[:, :], pbank[bank][0:64, :], csb[:, 0, :]),
                            waits=rw2, inc=("dve", 1))
                        E("dve", lambda v, bank=bank: v.tensor_mul(
                            rtb[:, :], pbank[bank][64:128, :], csb[:, 1, :]),
                            inc=("dve", 1))
                        E("dve", lambda v, qb=qb: v.tensor_sub(
                            qkst[0:64, qb, :], rta[:, :], rtb[:, :]),
                            waits=[("act", ev_sqn[qb])], inc=("dve", 1))
                        E("dve", lambda v, bank=bank: v.tensor_mul(
                            rta[:, :], pbank[bank][0:64, :], csb[:, 1, :]),
                            inc=("dve", 1))
                        E("dve", lambda v, bank=bank: v.tensor_mul(
                            rtb[:, :], pbank[bank][64:128, :], csb[:, 0, :]),
                            inc=("dve", 1))
                        ev_rope = E("dve", lambda v, qb=qb: v.tensor_add(
                            qkst[64:128, qb, :], rta[:, :], rtb[:, :]),
                            inc=("dve", 1))
                        prev["qk_dve"][qb] = ev_rope
                        idx = (p_ * 2 + s_) * NH + h
                        if h > 0:
                            emit_nrm(idx - 1, (h - 1) % 2)
                        ev_sqn[qb] = E("act", lambda a, qb=qb: a.activation(
                            sqb[:, qb, :], qkst[:, qb, :], AF.Square),
                            waits=[("dve", ev_rope),
                                   ("pe", ev_nrm_mm[qb])], inc=("act", 1))
                        dstt = qfs if p_ == 0 else kfs
                        E("gp", lambda g, h=h, qb=qb, s_=s_, trow=trow,
                          dstt=dstt: g.dma_start(
                              out=dstt[(s_ * NH + h) * P:
                                       (s_ * NH + h + 1) * P, trow:trow + T],
                              in_=qkst[:, qb, :]),
                            waits=[("dve", ev_rope)], inc=(dsem, 16))
                        prev["qk_dma"][qb] = plan.c(dsem)
                    emit_nrm((p_ * 2 + s_) * NH + NH - 1, (NH - 1) % 2)
                    prev["lnb_pe"] = plan.c("pe")
                    prev["csb_pe"] = plan.c("pe")
                    prev["nrm_dve"] = plan.c("dve")
                else:
                    for ts_ in range(4):
                        for dg in range(4):
                            i16 = ts_ * 4 + dg
                            bank = 4 + i16 % 2
                            mw = [(ressem[rb], ev_res[rb]), ev_ready,
                                  ("act", prev["vact"] if i16 >= 2 else 0),
                                  ("dve", prev["nrm_dve"] if i16 < 2 else 0)]
                            last = 0
                            for dc in range(DC):
                                last = E("pe", lambda t, ts_=ts_, dg=dg,
                                         dc=dc, bank=bank, rb=rb: t.matmul(
                                             pbank[bank][:],
                                             lnb[:, dc, ts_ * P:(ts_ + 1) * P],
                                             resv[rb][:, dc,
                                                      dg * T:(dg + 1) * T],
                                             start=(dc == 0),
                                             stop=(dc == DC - 1)),
                                         waits=(mw if dc == 0 else []),
                                         inc=("pe", 1))
                            ev_va = E("act", lambda a, i16=i16, bank=bank:
                                      a.activation(sqb[:, i16 % 2, :],
                                                   pbank[bank][:], AF.Copy),
                                      waits=[("pe", last),
                                             ("svt", prev["vdma"])],
                                      inc=("act", 1))
                            prev["vact"] = ev_va
                            E("gp", lambda g, ts_=ts_, dg=dg, s_=s_,
                              trow=trow, i16=i16: g.dma_start(
                                  out=vts[s_ * S + trow + ts_ * P:
                                          s_ * S + trow + (ts_ + 1) * P,
                                          dg * T:(dg + 1) * T],
                                  in_=sqb[:, i16 % 2, :]),
                                waits=[("act", ev_va)], inc=("svt", 16))
                            prev["vdma"] = plan.c("svt")
                    prev["lnb_pe"] = plan.c("pe")
            ev_res_pe[rb] = plan.c("pe")

        plan.barrier()

        E("dve", lambda v: v.tensor_mul(
            smt[0:1, 0:2 * NH], nrm[0:1, 0:2 * NH], nrm[0:1, 2 * NH:4 * NH]),
            inc=("dve", 1))
        ev_a2 = plan.c("dve")
        ev_sq_b = E("act", lambda a: a.activation(
            smt[0:1, 32:32 + 2 * NH], smt[0:1, 0:2 * NH], AF.Sqrt),
            waits=[("dve", ev_a2)], inc=("act", 1))
        ev_nb = E("dve", lambda v: v.tensor_scalar_mul(
            smt[0:1, 32:32 + 2 * NH], smt[0:1, 32:32 + 2 * NH], -1.0),
            waits=[("act", ev_sq_b)], inc=("dve", 1))
        ev_nbs = E("sy", lambda s: s.dma_start(
            out=miscd[2:3, 0:2 * NH], in_=smt[0:1, 32:32 + 2 * NH]),
            waits=[("dve", ev_nb)], inc=("snb", 16))
        E("sy", lambda s: s.dma_start(
            out=negB[:, :],
            in_=miscd[2:3, 0:2 * NH].broadcast_to((P, 2 * NH))),
            waits=[("snb", ev_nbs)], inc=("snb", 16))
        E("sy", lambda s: s.dma_start(out=mskb[:, :, :], in_=masks),
          inc=("kc", 16))
        ev_msk = plan.c("kc")
        ev_snb = plan.c("snb")

        if not attn_b:
            return
        ev_wo = {0: 0, 1: 0}
        ev_wope = {0: 0, 1: 0}
        pB = {"exp_act": {}, "pt_pe": {}, "af_dve": 0, "x_dve": 0,
              "lnb_pe2": 0, "s45_dve": 0, "cast_last": 0, "xa_gp": 0,
              "recip": 0, "linv_dve": 0, "stats_pe": 0, "evac_dve": 0,
              "qf_pe": 0, "kf_pe": {0: 0, 1: 0}, "vr_pe": 0}
        for s_ in range(NSEQ):
            E("gp", lambda g, s_=s_: g.dma_start(
                out=vrs[:, :, :],
                in_=vts[s_ * S:(s_ + 1) * S, :].rearrange(
                    "(c p) d -> p c d", p=P)),
                waits=[("pe", pB["vr_pe"])], inc=("svr", 16))
            ev_vr = plan.c("svr")
            for qt in range(TPS):
                trow = qt * T
                nkc = 4 * qt + 4
                for c in range(DC):
                    E("sy", lambda s, c=c, s_=s_, trow=trow: s.dma_start(
                        out=xa[:, c, :],
                        in_=x1f[s_ * DM + c * P:s_ * DM + (c + 1) * P,
                                trow:trow + T]),
                        waits=[("stx", pB["xa_gp"]), ("dve", pB["x_dve"]),
                               ("act", pB["cast_last"])], inc=("sx", 16))
                ev_x = plan.c("sx")
                for h in range(NH):
                    qb = h % 2
                    qsem = "qfa" if qb == 0 else "qfb"
                    ksem = "kfa" if qb == 0 else "kfb"
                    E("sy", lambda s, h=h, qb=qb, s_=s_, trow=trow:
                      s.dma_start(
                          out=qkst[:, qb, :],
                          in_=qfs[(s_ * NH + h) * P:(s_ * NH + h + 1) * P,
                                  trow:trow + T]),
                        waits=[("pe", pB["qf_pe"])], inc=(qsem, 16))
                    ev_qf = plan.c(qsem)
                    hk = nkc * P // 2
                    E("sy", lambda s, h=h, qb=qb, s_=s_, hk=hk:
                      s.dma_start(
                          out=kfv[:, qb, 0:hk],
                          in_=kfs[(s_ * NH + h) * P:(s_ * NH + h + 1) * P,
                                  0:hk]),
                        waits=[("pe", pB["kf_pe"][qb])], inc=(ksem, 16))
                    E("act", lambda s, h=h, qb=qb, s_=s_, hk=hk, nkc=nkc:
                      s.dma_start(
                          out=kfv[:, qb, hk:nkc * P],
                          in_=kfs[(s_ * NH + h) * P:(s_ * NH + h + 1) * P,
                                  hk:nkc * P]),
                        waits=[("pe", pB["kf_pe"][qb])], inc=(ksem, 16))
                    ev_kf = plan.c(ksem)
                    ev_l = 0
                    ev_o = 0
                    for kc in range(nkc):
                        bank = (0, 1, 5, 6)[kc % 4]
                        ev_st_ = E("pe", lambda t, kc=kc, qb=qb, bank=bank:
                                   t.matmul(pbank[bank][:],
                                            kfv[:, qb, kc * P:(kc + 1) * P],
                                            qkst[:, qb, :],
                                            start=True, stop=True),
                                   waits=[(ksem, ev_kf), (qsem, ev_qf),
                                          ("act",
                                           pB["exp_act"].get(kc % 4, 0))] +
                                         ([("dve", pB["evac_dve"])]
                                          if kc < 4 else []),
                                   inc=("pe", 1))
                        idx = s_ * NH + h
                        ev_e = E("act", lambda a, kc=kc, bank=bank, idx=idx:
                                 a.activation(ptb[:, kc % 4, :],
                                              pbank[bank][:], AF.Exp,
                                              bias=negB[:, idx:idx + 1]),
                                 waits=[("pe", ev_st_), ("snb", ev_snb),
                                        ("kc", ev_msk),
                                        ("pe", pB["pt_pe"].get(kc % 4, 0))],
                                 inc=("act", 1))
                        pB["exp_act"][kc % 4] = ev_e
                        ev_pt = ("act", ev_e)
                        if kc >= 4 * qt:
                            ev_mk = E("dve", lambda v, kc=kc, qt=qt:
                                      v.tensor_mul(ptb[:, kc % 4, :],
                                                   ptb[:, kc % 4, :],
                                                   mskb[:, kc - 4 * qt, :]),
                                      waits=[("act", ev_e)], inc=("dve", 1))
                            ev_pt = ("dve", ev_mk)
                        ev_l = E("pe", lambda t, kc=kc, nkc=nkc: t.matmul(
                            pbank[2][0:1, :], onesc[:], ptb[:, kc % 4, :],
                            start=(kc == 0), stop=(kc == nkc - 1)),
                            waits=[ev_pt,
                                   ("dve", pB["recip"] if kc == 0 else 0)],
                            inc=("pe", 1))
                        ev_o = E("pe", lambda t, kc=kc, h=h, nkc=nkc:
                                 t.matmul(pbank[3][:],
                                          vrs[:, kc, h * P:(h + 1) * P],
                                          ptb[:, kc % 4, :],
                                          start=(kc == 0),
                                          stop=(kc == nkc - 1)),
                                 waits=([("svr", ev_vr),
                                         ("dve", pB["af_dve"])]
                                        if kc == 0 else []),
                                 inc=("pe", 1))
                        pB["pt_pe"][kc % 4] = ev_o
                    E("dve", lambda v: v.reciprocal(
                        srow[0:1, 0, :], pbank[2][0:1, :]),
                        waits=[("pe", ev_l), ("sst", srow_free["ev"])],
                        inc=("dve", 1))
                    ev_r = E("dve", lambda v: v.tensor_copy(
                        linvb[:], srow[0:1, 0, :]), inc=("dve", 1))
                    pB["recip"] = ev_r
                    ev_lb = E("pe", lambda t: t.matmul(
                        pbank[4][:], onesr[:], linvb[:],
                        start=True, stop=True),
                        waits=[("dve", ev_r), ("dve", pB["linv_dve"]),
                               ("dve", pB["s45_dve"])], inc=("pe", 1))
                    ev_lc = E("dve", lambda v: v.tensor_copy(
                        linvr[:], pbank[4][:]),
                        waits=[("pe", ev_lb)], inc=("dve", 1))
                    pB["af_dve"] = E("dve", lambda v, h=h: v.tensor_mul(
                        afb[:, h, :], pbank[3][:], linvr[:]),
                        waits=[("pe", ev_o), ("dve", ev_lc)] +
                              ([("pe", pB["lnb_pe2"])] if h == 0 else []),
                        inc=("dve", 1))
                    pB["linv_dve"] = pB["af_dve"]
                    pB["qf_pe"] = plan.c("pe")
                    pB["kf_pe"][qb] = plan.c("pe")
                for half in range(2):
                    for h in range(NH):
                        wb = h % 2
                        wsem = "woa" if wb == 0 else "wob"
                        E("gp", lambda g, h=h, half=half, wb=wb: g.dma_start(
                            out=wov[:, wb, :],
                            in_=wot[h * P:(h + 1) * P,
                                    half * 1024:(half + 1) * 1024]),
                            waits=[("pe", ev_wope[wb])], inc=(wsem, 16))
                        ev_wo[wb] = plan.c(wsem)
                        last = 0
                        for dc8 in range(8):
                            last = E("pe", lambda t, h=h, dc8=dc8, wb=wb:
                                     t.matmul(
                                         pbank[dc8][:],
                                         wov[:, wb, dc8 * P:(dc8 + 1) * P],
                                         afb[:, h, :],
                                         start=(h == 0), stop=(h == NH - 1)),
                                     waits=([(wsem, ev_wo[wb]),
                                             ("dve", pB["af_dve"]),
                                             ("dve", pB["evac_dve"]),
                                             ("act", plan.c("act"))]
                                            if dc8 == 0 else []),
                                     inc=("pe", 1))
                        ev_wope[wb] = last
                    ev_a = 0
                    for dc8 in range(8):
                        d = half * 8 + dc8
                        ev_a = E("dve", lambda v, d=d, dc8=dc8: v.tensor_add(
                            xa[:, d, :], pbank[dc8][:], xa[:, d, :]),
                            waits=[("pe", ev_wope[1]), ("sx", ev_x)],
                            inc=("dve", 1))
                        pB["x_dve"] = ev_a
                        E("gp", lambda g, d=d, s_=s_, trow=trow: g.dma_start(
                            out=x2f[s_ * DM + d * P:s_ * DM + (d + 1) * P,
                                    trow:trow + T],
                            in_=xa[:, d, :]),
                            waits=[("dve", ev_a)], inc=("stx", 16))
                    pB["evac_dve"] = plan.c("dve")
                if upto >= 3:
                    ev_s2 = 0
                    for d in range(DC):
                        cw = [("dve", pB["evac_dve"])]
                        if d == 0:
                            cw += [("dve", pB["s45_dve"]),
                                   ("pe", pB["stats_pe"]),
                                   ("dve", pB["linv_dve"])]
                        ev_c = E("act", lambda a, d=d: a.activation(
                            lnb[:, d, :], xa[:, d, :], AF.Copy),
                            waits=cw, inc=("act", 1))
                        pB["cast_last"] = ev_c
                        evq = E("dve", lambda v, d=d: v.tensor_mul(
                            sqb[:, d % 2, :], lnb[:, d, :], lnb[:, d, :]),
                            waits=[("act", ev_c),
                                   ("pe", ev_s2 if d >= 2 else 0)],
                            inc=("dve", 1))
                        E("pe", lambda t, d=d: t.matmul(
                            pbank[4][0:1, :], onesc[:], lnb[:, d, :],
                            start=(d == 0), stop=(d == DC - 1)),
                            waits=[("act", ev_c), ("dve", pB["linv_dve"])],
                            inc=("pe", 1))
                        ev_s2 = E("pe", lambda t, d=d: t.matmul(
                            pbank[5][0:1, :], onesc[:], sqb[:, d % 2, :],
                            start=(d == 0), stop=(d == DC - 1)),
                            waits=[("dve", evq)], inc=("pe", 1))
                    finish_stats(s_ * TPS + qt, 0, ev_s2)
                    pB["s45_dve"] = plan.c("dve")
                    pB["stats_pe"] = plan.c("pe")
                pB["xa_gp"] = plan.c("stx")
                pB["lnb_pe2"] = plan.c("pe")
            pB["vr_pe"] = plan.c("pe")

    ev_pro = 0
    ev_pro_pe = 0
    ev_pro_act = 0
    for ti in range(NT):
        s_, trow = ti // TPS, (ti % TPS) * T
        for c in range(DC):
            E("act", lambda s, c=c, s_=s_, trow=trow: s.dma_start(
                out=xa[:, 8 + c % 8, :],
                in_=x0f[s_ * DM + c * P:s_ * DM + (c + 1) * P,
                        trow:trow + T]),
                waits=[("act", ev_pro_act), ("sst", srow_free["ev"]),
                       ("dve", plan.c("dve") if ti == 0 and c == 0 else 0)],
                inc=("sx", 16)) if c < 8 else None
        ev_mm2 = 0
        ev_sqp = [0] * DC
        for half in range(2):
            hw = [("sx", plan.c("sx"))]
            for c in range(8):
                cc_ = half * 8 + c
                ev_castp = E("act", lambda a, c=c, cc_=cc_: a.activation(
                    qkst[:, cc_ % 2, :], xa[:, 8 + c, :], AF.Copy),
                    waits=hw + [("pe", ev_sqp[cc_ - 2] if cc_ >= 2 else
                                 (ev_pro_pe if ti else 0))],
                    inc=("act", 1))
                evq = E("dve", lambda v, cc_=cc_: v.tensor_mul(
                    sqb[:, cc_ % 2, :], qkst[:, cc_ % 2, :],
                    qkst[:, cc_ % 2, :]),
                    waits=[("act", ev_castp),
                           ("pe", ev_sqp[cc_ - 2] if cc_ >= 2 else
                            (ev_pro_pe if ti else 0))],
                    inc=("dve", 1))
                E("pe", lambda t, cc_=cc_: t.matmul(
                    pbank[4][0:1, :], onesc[:], qkst[:, cc_ % 2, :],
                    start=(cc_ == 0), stop=(cc_ == DC - 1)),
                    waits=[("act", ev_castp), ("dve", ev_ones)],
                    inc=("pe", 1))
                ev_sqp[cc_] = ev_mm2 = E("pe", lambda t, cc_=cc_: t.matmul(
                    pbank[5][0:1, :], onesc[:], sqb[:, cc_ % 2, :],
                    start=(cc_ == 0), stop=(cc_ == DC - 1)),
                    waits=[("dve", evq)], inc=("pe", 1))
            if half == 0:
                for c in range(8):
                    E("act", lambda s, c=c, s_=s_, trow=trow: s.dma_start(
                        out=xa[:, 8 + c, :],
                        in_=x0f[s_ * DM + (8 + c) * P:
                                s_ * DM + (9 + c) * P, trow:trow + T]),
                        waits=[("act", plan.c("act"))], inc=("sx", 16))
        ev_pro = finish_stats(ti, 0, ev_mm2)
        ev_pro_pe = plan.c("pe")
        ev_pro_act = plan.c("act")
    ev_c12_1 = sqrt_epoch(0, ev_pro)

    quantize(w1t, w1q, DC, FF, w1cb, 0, DM * FF, "p0a", "p0b", "t1s", "t1")
    ev_wq = quantize(w2t, w2q, FC, DM, w2cb, 1, DM * FF, "p0c", "p0d",
                     "t2s", "t2")

    plan.barrier()
    dst1 = yf if upto == 1 else x1f
    ffn_phase(x0f, dst1, 0, 1, do_stats=(upto > 1), ev_c12_ready=ev_c12_1)

    if upto >= 2:
        plan.barrier()
        ev_c12_2 = sqrt_epoch(1, plan.c("sst"))
        attn_phase(ev_c12_2)
        plan.barrier()
        if upto == 2:
            srcf = x2f if attn_b else x1f
            for c in range(2 * DC):
                E("sy", lambda s, c=c, srcf=srcf: s.dma_start(
                    out=yf[c * P:(c + 1) * P, :],
                    in_=srcf[c * P:(c + 1) * P, :]),
                    inc=("stx", 16))
        else:
            ev_c12_3 = sqrt_epoch(0, plan.c("sst"))
            ffn_phase(x2f, yf, 0, 1, do_stats=False, ev_c12_ready=ev_c12_3)

    plan.barrier()

    if debug:
        for name, ap in [("d_w1q", w1q), ("d_w2q", w2q), ("d_stats", statsd),
                         ("d_c12", c12d), ("d_misc", miscd), ("d_x1", x1f), ("d_x2", x2f),
                         ("d_ln2", ln2f), ("d_qf", qfs), ("d_kf", kfs),
                         ("d_vt", vts)]:
            o = nc.dram_tensor(name, list(ap.shape), ap.dtype,
                               kind="ExternalOutput").ap()
            n = ap.shape[0]
            for r0 in range(0, n, 2048):
                r1 = min(n, r0 + 2048)
                E("sy", lambda s, o=o, ap=ap, r0=r0, r1=r1: s.dma_start(
                    out=o[r0:r1, :], in_=ap[r0:r1, :]), inc=("sdbg", 16))
        o = nc.dram_tensor("d_nrm", [1, 4 * NH], F32,
                           kind="ExternalOutput").ap()
        E("sy", lambda s, o=o: s.dma_start(out=o, in_=nrm[0:1, :]),
          inc=("sdbg", 16))
        o2 = nc.dram_tensor("d_smt", [1, 64], F32, kind="ExternalOutput").ap()
        E("sy", lambda s, o2=o2: s.dma_start(out=o2, in_=smt[0:1, :]),
          inc=("sdbg", 16))
        o3 = nc.dram_tensor("d_negB", [2, 2 * NH], F32,
                            kind="ExternalOutput").ap()
        E("sy", lambda s, o3=o3: s.dma_start(out=o3, in_=negB[0:2, :]),
          inc=("sdbg", 16))
        plan.barrier()

    block = ctx.enter_context(nc.Block())
    for eng, attr in (("sy", "sync"), ("gp", "gpsimd"), ("pe", "tensor"),
                      ("dve", "vector"), ("act", "scalar")):
        def body(obj, eng=eng):
            for f in plan.thunks[eng]:
                f(obj)
        getattr(block, attr)(body)
    ctx.close()
    return nc


def host_inputs(L0_attn_x, L0_attn_Wq, L0_attn_Wk, L0_attn_Wv, L0_attn_Wo,
                L0_ffn_w1, L0_ffn_b1, L0_ffn_w2, L0_ffn_b2):
    x = np.asarray(L0_attn_x, np.float32)
    deint = np.concatenate(
        [np.concatenate([np.arange(h * 128, (h + 1) * 128, 2),
                         np.arange(h * 128 + 1, (h + 1) * 128, 2)])
         for h in range(NH)])
    com = {
        "w1t": np.ascontiguousarray(np.asarray(L0_ffn_w1, np.float32).T),
        "w2t": np.ascontiguousarray(np.asarray(L0_ffn_w2, np.float32).T),
        "b1b": np.ascontiguousarray(
            np.asarray(L0_ffn_b1, np.float32).reshape(FC, P).T),
        "b2b": np.ascontiguousarray(
            np.asarray(L0_ffn_b2, np.float32).reshape(DC, P).T),
        "wqt": np.ascontiguousarray(np.asarray(L0_attn_Wq, np.float32)[deint].T),
        "wkt": np.ascontiguousarray(np.asarray(L0_attn_Wk, np.float32)[deint].T),
        "wvt": np.ascontiguousarray(np.asarray(L0_attn_Wv, np.float32).T),
        "wot": np.ascontiguousarray(np.asarray(L0_attn_Wo, np.float32).T),
    }
    invf = 1.0 / (10000.0 ** (np.arange(0, 128, 2, dtype=np.float32)
                              / np.float32(128.0)))
    fr = np.outer(invf.astype(np.float32), np.arange(S, dtype=np.float32))
    cos = np.cos(fr).astype(np.float32)
    sin = np.sin(fr).astype(np.float32)
    sc = np.float32(1.0 / np.sqrt(np.float32(128.0)))
    com["cost"] = np.ascontiguousarray(np.stack([cos, sin], 1))
    com["costq"] = np.ascontiguousarray(np.stack([cos * sc, sin * sc], 1))
    kk = np.arange(P)[:, None, None]
    jj = np.arange(4)[None, :, None]
    qq = np.arange(T)[None, None, :]
    com["masks"] = ((jj * P + kk) <= qq).astype(ml_dtypes.bfloat16)
    in_maps = []
    for c in range(NCORES):
        xc = np.ascontiguousarray(
            x[2 * c:2 * c + 2].transpose(0, 2, 1)).reshape(NSEQ * DM, S)
        in_maps.append(dict(com, x0f=xc))
    return in_maps


def kernel(**inputs):
    in_maps = host_inputs(**inputs)
    nc = build_program(UPTO, DEBUG, ATTN_B, ATTN_NP)
    res = run_bass_kernel_spmd(nc, in_maps, core_ids=list(range(NCORES)))
    out = np.empty((16, S, DM), np.float32)
    for c in range(NCORES):
        yfc = res.results[c]["yf"].reshape(NSEQ, DM, S)
        out[2 * c] = yfc[0].T
        out[2 * c + 1] = yfc[1].T
    if DEBUG:
        kernel.dbg = res.results
    return out
